# version 09
# baseline (speedup 1.0000x reference)
import bisect
import contextlib

import numpy as np

import concourse.bass as bass
import concourse.mybir as mybir
from concourse.bass_utils import run_bass_kernel_spmd

F32 = mybir.dt.float32
BF16 = mybir.dt.bfloat16
AF = mybir.ActivationFunctionType
ALU = mybir.AluOpType
AX = mybir.AxisListType

E = 1024
H = 16
DH = 64
NEXP = 8
INTER = 1024
SHI = 2048
EPS_LN = 1e-5
EPS_RMS = 1e-5

CFG_FULL = dict(B=2, NBLK=4, TBLK=1024)
TSUB = 512

ENGS = ("sync", "gpsimd", "tensor", "scalar", "vector")


class P:

    def __init__(self, nc):
        self.nc = nc
        self.eng = {"sync": nc.sync, "gpsimd": nc.gpsimd, "tensor": nc.tensor,
                    "scalar": nc.scalar, "vector": nc.vector}
        self.sem = {k: nc.alloc_semaphore(name="sem_" + k) for k in ENGS}
        self.cnt = {k: 0 for k in ENGS}
        self.dsem = {}
        self.dcnt = {}
        self.clock = {k: {j: 0 for j in ENGS} for k in ENGS}
        self.hist = {k: ([0], [dict(self.clock[k])]) for k in ENGS}
        self.last_w = {}
        self.readers = {}
        self.ivals = {}
        self._olap_cache = {}
        self.n_emit = 0
        self.n_wait = 0

    def reg(self, key, start, end):
        self.ivals[key] = (start, end)
        self._olap_cache = {}

    def _olap(self, key):
        s = self._olap_cache.get(key)
        if s is not None:
            return s
        s = []
        iv = self.ivals.get(key)
        if iv is not None:
            for k2, iv2 in self.ivals.items():
                if k2 != key and iv2[0] < iv[1] and iv[0] < iv2[1]:
                    s.append(k2)
        self._olap_cache[key] = s
        return s

    def _getdsem(self, name):
        if name not in self.dsem:
            self.dsem[name] = self.nc.alloc_semaphore(name="dq_" + name)
            self.dcnt[name] = 0
        return self.dsem[name]

    def bump(self, keys, dname):
        v = self.dcnt[dname]
        for k in keys:
            self.last_w[k] = (("D", dname), v)

    def emit(self, eng, fn, reads=(), writes=(), dma=False, dsem=None):
        waits = {}
        clk = self.clock[eng]

        def need(f, v):
            if clk.get(f, 0) < v:
                if f == "tensor" and eng == "tensor":
                    return
                if waits.get(f, 0) < v:
                    waits[f] = v

        def need_w(b):
            w = self.last_w.get(b)
            if w is not None:
                need(*w)

        def need_r(b):
            for f, v in self.readers.get(b, {}).items():
                need(f, v)

        for b in reads:
            need_w(b)
            for b2 in self._olap(b):
                need_w(b2)
        for b in writes:
            need_w(b)
            need_r(b)
            for b2 in self._olap(b):
                need_w(b2)
                need_r(b2)
        Eo = self.eng[eng]
        for f in sorted(waits, key=str):
            v = waits[f]
            if isinstance(f, tuple):
                Eo.wait_ge(self.dsem[f[1]], v)
            else:
                Eo.wait_ge(self.sem[f], v)
            self.n_wait += 1
            if clk.get(f, 0) < v:
                clk[f] = v
            if not isinstance(f, tuple):
                cnts, clocks = self.hist[f]
                i = bisect.bisect_right(cnts, v) - 1
                if i >= 0:
                    for g, gv in clocks[i].items():
                        if clk.get(g, 0) < gv:
                            clk[g] = gv
        inst = fn(Eo)
        if dma:
            assert dsem is not None
            self._getdsem(dsem)
            self.dcnt[dsem] += 16
            inst.then_inc(self.dsem[dsem], 16)
            who = ("D", dsem)
            new = self.dcnt[dsem]
        else:
            self.cnt[eng] += 1
            inst.then_inc(self.sem[eng], 1)
            who = eng
            new = self.cnt[eng]
            cnts, clocks = self.hist[eng]
            cnts.append(new)
            clocks.append(dict(clk))
        for b in writes:
            self.last_w[b] = (who, new)
            self.readers[b] = {}
        for b in reads:
            self.readers.setdefault(b, {})[eng if not dma else who] = new
        self.n_emit += 1
        return new


def build(cfg):
    B = cfg["B"]
    NBLK = cfg["NBLK"]
    TBLK = cfg["TBLK"]
    T = NBLK * TBLK
    NTT = TBLK // 512
    NSUB = TBLK // TSUB

    nc = bass.Bass("TRN2", target_bir_lowering=False, debug=False)

    x_d = nc.dram_tensor("x", [B, E, T], F32, kind="ExternalInput").ap()
    y_d = nc.dram_tensor("y", [B, E, T], F32, kind="ExternalOutput").ap()
    w1T_d = nc.dram_tensor("w1T", [NEXP, E, INTER], F32, kind="ExternalInput").ap()
    w3T_d = nc.dram_tensor("w3T", [NEXP, E, INTER], F32, kind="ExternalInput").ap()
    w2T_d = nc.dram_tensor("w2T", [NEXP, INTER, E], F32, kind="ExternalInput").ap()
    sw1T_d = nc.dram_tensor("sw1T", [E, SHI], F32, kind="ExternalInput").ap()
    sw3T_d = nc.dram_tensor("sw3T", [E, SHI], F32, kind="ExternalInput").ap()
    sw2T_d = nc.dram_tensor("sw2T", [SHI, E], F32, kind="ExternalInput").ap()
    gwT_d = nc.dram_tensor("gwT", [E, NEXP], F32, kind="ExternalInput").ap()
    gbias_d = nc.dram_tensor("gbias", [1, NEXP], F32, kind="ExternalInput").ap()
    wq_d = nc.dram_tensor("wq", [E, E], F32, kind="ExternalInput").ap()
    wk_d = nc.dram_tensor("wk", [E, E], F32, kind="ExternalInput").ap()
    wv_d = nc.dram_tensor("wv", [E, E], F32, kind="ExternalInput").ap()
    wo_d = nc.dram_tensor("wo", [E, E], F32, kind="ExternalInput").ap()
    g1_d = nc.dram_tensor("g1", [E, DH], F32, kind="ExternalInput").ap()
    g2_d = nc.dram_tensor("g2", [DH, E], F32, kind="ExternalInput").ap()
    deltar_d = nc.dram_tensor("deltar", [8, 128], F32, kind="ExternalInput").ap()
    deltaf_d = nc.dram_tensor("deltaf", [1, E], F32, kind="ExternalInput").ap()
    normwr_d = nc.dram_tensor("normwr", [8, 128], F32, kind="ExternalInput").ap()
    tri_d = nc.dram_tensor("ctri", [128, 128], F32, kind="ExternalInput").ap()
    idn_d = nc.dram_tensor("cidn", [128, 128], F32, kind="ExternalInput").ap()
    oneh_d = nc.dram_tensor("coneh", [8, NEXP * 128], F32, kind="ExternalInput").ap()

    z_w1 = nc.dram_tensor("z_w1", [NEXP, 128, E * INTER // 128], BF16).ap()
    z_w3 = nc.dram_tensor("z_w3", [NEXP, 128, E * INTER // 128], BF16).ap()
    z_w2 = nc.dram_tensor("z_w2", [NEXP, 128, INTER * E // 128], BF16).ap()
    z_sw1 = nc.dram_tensor("z_sw1", [128, E * SHI // 128], BF16).ap()
    z_sw3 = nc.dram_tensor("z_sw3", [128, E * SHI // 128], BF16).ap()
    z_sw2 = nc.dram_tensor("z_sw2", [128, SHI * E // 128], BF16).ap()
    z_wq = nc.dram_tensor("z_wq", [128, E * E // 128], BF16).ap()
    z_wk = nc.dram_tensor("z_wk", [128, E * E // 128], BF16).ap()
    z_wv = nc.dram_tensor("z_wv", [128, E * E // 128], BF16).ap()
    z_wo = nc.dram_tensor("z_wo", [128, E * E // 128], BF16).ap()

    ctx = contextlib.ExitStack()

    def sbt(name, shape, dtype):
        return ctx.enter_context(nc.sbuf_tensor(name, shape, dtype)).ap()

    xr = sbt("xr", [128, 8, TBLK], F32)
    xh = sbt("xh", [128, 8, TBLK], BF16)
    S = sbt("S", [128, 8, DH], F32)
    tri_f = sbt("tri_f", [128, 128], F32)
    idn_f = sbt("idn_f", [128, 128], F32)
    tri_b = sbt("tri_b", [128, 128], BF16)
    trib_b = sbt("trib_b", [128, 128], BF16)
    idn_b = sbt("idn_b", [128, 128], BF16)
    delta_c = sbt("delta_c", [128, 8], F32)
    ndelta_c = sbt("ndelta_c", [128, 8], F32)
    normw_c = sbt("normw_c", [128, 8], F32)
    bias_bc = sbt("bias_bc", [128, 8], F32)
    delta_bc = sbt("delta_bc", [128, 2, 512], F32)
    g1sb = sbt("g1sb", [128, 8, DH], BF16)
    g2sb = sbt("g2sb", [DH, E], BF16)
    gwsb = sbt("gwsb", [128, 8, 8], BF16)
    onehb = sbt("onehb", [8, NEXP * 128], BF16)
    grow = sbt("grow", [8, 8 * 128], BF16)
    rows = sbt("rows", [1, 2048], F32)
    srows = sbt("srows", [1, 1032], F32)
    gsm = sbt("gsm", [128, 64], F32)
    gsb8 = sbt("gsb8", [128, 8], BF16)
    lam = sbt("lam", [128, 8], F32)
    sc0 = sbt("sc0", [128, 512], F32)
    sc1 = sbt("sc1", [128, 512], F32)
    sc2 = sbt("sc2", [128, 512], F32)
    sc3 = sbt("sc3", [128, 512], F32)
    cst = sbt("cst", [128, 2], F32)
    tri4 = sbt("tri4", [128, 512], F32)
    arena = ctx.enter_context(nc.sbuf_tensor("arena", [128, 78336], mybir.dt.uint8)).ap()

    ones_col_f = tri_f[:, 127:128]
    ones_row_f = tri_f[0:1, :]
    ones_col_b = tri_b[:, 127:128]

    prog = P(nc)

    def carve(off, shape, dtype, key=None, subkeys=None):
        n = 1
        for s in shape:
            n *= s
        esz = 2 if dtype == BF16 else 4
        nbytes = n * esz
        v = arena[:, off:off + nbytes].bitcast(dtype)
        if len(shape) > 1:
            spec = "p (" + " ".join(f"a{i}" for i in range(len(shape))) + ") -> p " + \
                   " ".join(f"a{i}" for i in range(len(shape)))
            kw = {f"a{i}": s for i, s in enumerate(shape)}
            v = v.rearrange(spec, **kw)
        if key is not None:
            prog.reg(key, off, off + nbytes)
        if subkeys is not None:
            prefix, nslice = subkeys
            sl = nbytes // nslice
            for i in range(nslice):
                prog.reg((prefix, i), off + i * sl, off + (i + 1) * sl)
        return v

    hg = carve(0, [8, 1024], BF16, subkeys=("hg", 8))
    hgs = carve(0, [16, 512], BF16)
    YA0 = 16384
    ya = carve(YA0, [8, TBLK], F32, subkeys=("ya", 8))
    W2S0 = YA0 + TBLK * 32
    w2s = carve(W2S0, [8, 1024], BF16, subkeys=("w2s", 8))
    w2sf = carve(W2S0, [8192], BF16)
    WOFF = W2S0 + 16384
    wu = carve(WOFF, [2, 8, 128], BF16, subkeys=("wu", 2))
    wv3 = carve(WOFF + 4096, [2, 8, 128], BF16, subkeys=("wv3", 2))
    g_sb = carve(49152 + 16384 + 8192, [2, 512], F32, key="g_sb")
    qs = carve(0, [8, TSUB], BF16, subkeys=("qs", 8))
    kf = carve(8192, [8, TSUB], BF16, subkeys=("kf", 8))
    kt = carve(16384, [4, E], BF16, subkeys=("kt", 4))
    vt = carve(24576, [4, E], BF16, subkeys=("vt", 4))
    gt = carve(32768, [4, E], BF16, subkeys=("gt", 4))
    of = carve(40960, [8, TSUB], BF16, subkeys=("of", 8))
    ex0 = carve(49152, [512], F32, key="ex0")
    ex1 = carve(51200, [512], F32, key="ex1")
    ex2 = carve(53248, [512], F32, key="ex2")
    qtb = carve(55296, [4, 128], BF16, key="qtb")
    khb = carve(56320, [4, 128], BF16, key="khb")
    Ktb = carve(57344, [4, 128], BF16, key="Ktb")
    Sbb = carve(58368, [4, DH], BF16, key="Sbb")
    Pm = carve(58880, [8, 128], BF16, key="Pm")
    Pmf = carve(58880, [1024], BF16, key="Pm")
    wat = carve(60928, [8, 512], BF16, subkeys=("wat", 8))
    wat2 = carve(69120, [2, 8, 128], BF16, subkeys=("wat2", 2))
    og = carve(0, [8, TSUB], BF16, subkeys=("og", 8))
    on = carve(8192, [8, TSUB], BF16, subkeys=("on", 8))
    g1a = carve(73216, [512], BF16, key="g1a")
    stg = [carve(0, [16384], BF16, key="stg0"),
           carve(32768, [16384], BF16, key="stg1")]

    PB = [ctx.enter_context(nc.psum_tensor(f"pb{i}", [128, 512], F32)).ap()
          for i in range(8)]
    BK = [f"B{i}" for i in range(8)]

    def mm(bank_ap, pairs, bkey, reads, start0=True, stopN=True):
        def fn(Eo):
            n = len(pairs)
            ii = None
            for i, (lhsT, rhs) in enumerate(pairs):
                ii = Eo.matmul(bank_ap, lhsT, rhs,
                               start=(i == 0 and start0), stop=(i == n - 1 and stopN))
            return ii
        return prog.emit("tensor", fn, reads=reads, writes=[bkey])

    def dma(eng, out, in_, reads, writes, dsem):
        prog.emit(eng, lambda Eo: Eo.dma_start(out=out, in_=in_),
                  reads=reads, writes=writes, dma=True, dsem=dsem)

    prog.emit("vector", lambda Eo: Eo.memset(cst[:, 0:1], EPS_LN),
              reads=[], writes=["cst"])
    prog.emit("vector", lambda Eo: Eo.memset(cst[:, 1:2], 1.0),
              reads=[], writes=["cst"])
    dma("sync", tri_f[:, :], tri_d, [], ["tri_f"], "setup")
    dma("sync", idn_f[:, :], idn_d, [], ["idn_f"], "setup")
    dma("sync", sc0[0:8, 0:128], deltar_d, [], ["sc0"], "setup")
    dma("sync", sc1[0:8, 0:128], normwr_d, [], ["sc1"], "setup")
    dma("sync", srows[0:1, 0:8], gbias_d, [], ["sr_gb"], "setup")
    dma("sync", srows[0:1, 8:8 + E], deltaf_d, [], ["sr_df"], "setup")
    prog.bump(["tri_f", "idn_f", "sc0", "sc1", "sr_gb", "sr_df"], "setup")
    for k4 in range(4):
        prog.emit("scalar", lambda Eo, k4=k4: Eo.copy(tri4[:, k4 * 128:(k4 + 1) * 128],
                                                      tri_f[:, :]),
                  reads=["tri_f"], writes=["tri4"])
    prog.emit("vector", lambda Eo: Eo.tensor_copy(tri_b[:, :], tri_f[:, :]),
              reads=["tri_f"], writes=["tri_b"])
    prog.emit("vector", lambda Eo: Eo.tensor_scalar(trib_b[:, :], tri_f[:, :], 1.0, -1.0,
                                                    ALU.subtract, ALU.mult),
              reads=["tri_f"], writes=["trib_b"])
    prog.emit("vector", lambda Eo: Eo.tensor_copy(idn_b[:, :], idn_f[:, :]),
              reads=["idn_f"], writes=["idn_b"])

    prog.emit("tensor", lambda Eo: Eo.transpose(PB[0][:, 0:8], sc0[0:8, 0:128],
                                                idn_f[0:8, 0:8]),
              reads=["sc0", "idn_f"], writes=["B0"])
    prog.emit("scalar", lambda Eo: Eo.copy(delta_c[:, :], PB[0][:, 0:8]),
              reads=["B0"], writes=["delta_c"])
    prog.emit("tensor", lambda Eo: Eo.transpose(PB[1][:, 0:8], sc1[0:8, 0:128],
                                                idn_f[0:8, 0:8]),
              reads=["sc1", "idn_f"], writes=["B1"])
    prog.emit("scalar", lambda Eo: Eo.copy(normw_c[:, :], PB[1][:, 0:8]),
              reads=["B1"], writes=["normw_c"])
    prog.emit("vector", lambda Eo: Eo.tensor_scalar_mul(ndelta_c[:, :], delta_c[:, :], -1.0),
              reads=["delta_c"], writes=["ndelta_c"])

    prog.emit("tensor", lambda Eo: Eo.matmul(PB[2][:, 0:8], ones_row_f, srows[0:1, 0:8],
                                             start=True, stop=True),
              reads=["sr_gb", "tri_f"], writes=["B2"])
    prog.emit("scalar", lambda Eo: Eo.copy(bias_bc[:, :], PB[2][:, 0:8]),
              reads=["B2"], writes=["bias_bc"])
    for hf in range(2):
        prog.emit("tensor",
                  lambda Eo, hf=hf: Eo.matmul(PB[3][:, 0:512], ones_row_f,
                                              srows[0:1, 8 + hf * 512:8 + (hf + 1) * 512],
                                              start=True, stop=True),
                  reads=["sr_df", "tri_f"], writes=["B3"])
        prog.emit("scalar", lambda Eo, hf=hf: Eo.copy(delta_bc[:, hf, :], PB[3][:, 0:512]),
                  reads=["B3"], writes=["delta_bc"])

    dma("gpsimd", g1sb[:, :, :], g1_d.rearrange("(c p) j -> p c j", p=128),
        [], ["g1sb"], "smallw")
    dma("gpsimd", g2sb[:, :], g2_d, [], ["g2sb"], "smallw")
    dma("gpsimd", gwsb[:, :, :], gwT_d.rearrange("(c p) e -> p c e", p=128),
        [], ["gwsb"], "smallw")
    dma("gpsimd", onehb[:, :], oneh_d, [], ["onehb"], "smallw")
    prog.bump(["g1sb", "g2sb", "gwsb", "onehb"], "smallw")

    conv_list = [
        (sw1T_d.rearrange("(c p) i -> p c i", p=128), z_sw1, "zsw1"),
        (sw3T_d.rearrange("(c p) i -> p c i", p=128), z_sw3, "zsw3"),
        (sw2T_d.rearrange("(ic p) o -> p ic o", p=128), z_sw2, "zsw2"),
    ]
    for e in range(NEXP):
        conv_list.append((w1T_d[e].rearrange("(c p) i -> p c i", p=128),
                          z_w1[e], ("zw1", e)))
        conv_list.append((w3T_d[e].rearrange("(c p) i -> p c i", p=128),
                          z_w3[e], ("zw3", e)))
        conv_list.append((w2T_d[e].rearrange("(ic p) o -> p ic o", p=128),
                          z_w2[e], ("zw2", e)))
    conv_list += [
        (wq_d.rearrange("(c p) j -> p c j", p=128), z_wq, "zwq"),
        (wk_d.rearrange("(c p) j -> p c j", p=128), z_wk, "zwk"),
        (wv_d.rearrange("(c p) j -> p c j", p=128), z_wv, "zwv"),
        (wo_d.rearrange("(c p) j -> p c j", p=128), z_wo, "zwo"),
    ]
    zslot = {}
    ci = 0
    for src_, dst, zkey in conv_list:
        a, bdim = src_.shape[1], src_.shape[2]
        n = a * bdim
        assert n <= 16384, (a, bdim)
        s = ci % 2
        stv = stg[s][:, 0:n].rearrange("p (a b) -> p a b", a=a)
        dma("gpsimd", stv, src_, [], [f"stg{s}"], f"stgq{s}")
        dma("gpsimd", dst[:, 0:n], stg[s][:, 0:n], [f"stg{s}"], [zkey], f"zst{s}")
        zslot[zkey] = f"zst{s}"
        ci += 1

    def emit_ln():
        for tt in range(NTT):
            ts = slice(tt * 512, tt * 512 + 512)
            for c in range(8):
                prog.emit("scalar",
                          lambda Eo, c=c, ts=ts: Eo.activation(xh[:, c, ts], xr[:, c, ts],
                                                               AF.Square),
                          reads=[("xr", c)], writes=[("xh", c)])
            mm(PB[0][0:1, 0:512], [(ones_col_f, xr[:, c, ts]) for c in range(8)],
               "B0", [("xr", c) for c in range(8)] + ["tri_f"])
            mm(PB[1][0:1, 0:512], [(ones_col_b, xh[:, c, ts]) for c in range(8)],
               "B1", [("xh", c) for c in range(8)] + ["tri_b"])
            r_m = rows[0:1, 0:512]
            r_q = rows[0:1, 512:1024]
            r_a = rows[0:1, 1024:1536]
            r_b = rows[0:1, 1536:2048]
            prog.emit("scalar", lambda Eo: Eo.mul(r_m, PB[0][0:1, 0:512], 1.0 / E),
                      reads=["B0"], writes=["r_m"])
            prog.emit("scalar", lambda Eo: Eo.mul(r_q, PB[1][0:1, 0:512], 1.0 / E),
                      reads=["B1"], writes=["r_q"])
            prog.emit("vector", lambda Eo: Eo.scalar_tensor_tensor(r_a, r_m, 1.0, r_m,
                                                                   ALU.mult, ALU.mult),
                      reads=["r_m"], writes=["r_a"])
            prog.emit("vector", lambda Eo: Eo.tensor_sub(r_q, r_q, r_a),
                      reads=["r_q", "r_a"], writes=["r_q"])
            prog.emit("scalar", lambda Eo: Eo.activation(r_a, r_q, AF.Sqrt,
                                                         bias=cst[0:1, 0:1]),
                      reads=["r_q", "cst"], writes=["r_a"])
            prog.emit("vector", lambda Eo: Eo.reciprocal(r_a, r_a),
                      reads=["r_a"], writes=["r_a"])
            prog.emit("vector", lambda Eo: Eo.scalar_tensor_tensor(r_b, r_m, -1.0, r_a,
                                                                   ALU.mult, ALU.mult),
                      reads=["r_m", "r_a"], writes=["r_b"])
            prog.emit("tensor", lambda Eo: Eo.matmul(PB[2][:, 0:512], ones_row_f, r_a,
                                                     start=True, stop=True),
                      reads=["r_a", "tri_f"], writes=["B2"])
            prog.emit("tensor", lambda Eo: Eo.matmul(PB[3][:, 0:512], ones_row_f, r_b,
                                                     start=True, stop=True),
                      reads=["r_b", "tri_f"], writes=["B3"])
            for c in range(8):
                prog.emit("vector",
                          lambda Eo, c=c, ts=ts: Eo.tensor_mul(sc0[:, :], xr[:, c, ts],
                                                               PB[2][:, 0:512]),
                          reads=[("xr", c), "B2"], writes=["sc0"])
                prog.emit("vector",
                          lambda Eo, c=c, ts=ts: Eo.tensor_add(xh[:, c, ts], sc0[:, :],
                                                               PB[3][:, 0:512]),
                          reads=["sc0", "B3"], writes=[("xh", c)])

    def emit_gating(only_tt=None):
        st = gsm[:, 0:8]
        mx = gsm[:, 8:9]
        sm = gsm[:, 9:10]
        stb = gsm[:, 16:24]
        mx8 = gsm[:, 24:32]
        m2 = gsm[:, 25:26]
        msk = gsm[:, 32:40]
        gts = gsm[:, 40:48]
        for tt in ([only_tt] if only_tt is not None else range(NTT)):
            ts = slice(tt * 512, tt * 512 + 512)
            mm(PB[2][0:8, 0:512], [(gwsb[:, c, :], xh[:, c, ts]) for c in range(8)],
               "B2", [("xh", c) for c in range(8)] + ["gwsb"])
            prog.emit("scalar", lambda Eo: Eo.copy(sc0[0:8, 0:512], PB[2][0:8, 0:512]),
                      reads=["B2"], writes=["sc0"])
            for q in range(4):
                tc = tt * 4 + q
                prog.emit("tensor",
                          lambda Eo, q=q: Eo.transpose(PB[3][:, 0:8],
                                                       sc0[0:8, q * 128:(q + 1) * 128],
                                                       idn_f[0:8, 0:8]),
                          reads=["sc0", "idn_f"], writes=["B3"])
                prog.emit("scalar", lambda Eo: Eo.copy(st, PB[3][:, 0:8]),
                          reads=["B3"], writes=["g_st"])
                prog.emit("vector", lambda Eo: Eo.reduce_max(mx, st, AX.X),
                          reads=["g_st"], writes=["g_mx"])
                prog.emit("vector", lambda Eo: Eo.tensor_scalar(st, st, mx, None, ALU.subtract),
                          reads=["g_st", "g_mx"], writes=["g_st"])
                prog.emit("scalar", lambda Eo: Eo.activation(st, st, AF.Exp),
                          reads=["g_st"], writes=["g_st"])
                prog.emit("vector", lambda Eo: Eo.reduce_sum(sm, st, AX.X),
                          reads=["g_st"], writes=["g_sm"])
                prog.emit("vector", lambda Eo: Eo.reciprocal(sm, sm),
                          reads=["g_sm"], writes=["g_sm"])
                prog.emit("vector", lambda Eo: Eo.tensor_scalar(st, st, sm, None, ALU.mult),
                          reads=["g_st", "g_sm"], writes=["g_st"])
                prog.emit("vector", lambda Eo: Eo.tensor_add(stb, st, bias_bc[:, :]),
                          reads=["g_st", "bias_bc"], writes=["g_stb"])
                prog.emit("vector", lambda Eo: Eo.max(out=mx8, in_=stb),
                          reads=["g_stb"], writes=["g_mx8"])
                prog.emit("vector", lambda Eo: Eo.tensor_scalar(msk, stb, m2, None, ALU.is_ge),
                          reads=["g_stb", "g_mx8"], writes=["g_msk"])
                prog.emit("vector", lambda Eo: Eo.tensor_mul(gts, st, msk),
                          reads=["g_st", "g_msk"], writes=["g_gts"])
                prog.emit("vector", lambda Eo: Eo.tensor_copy(gsb8[:, :], gts),
                          reads=["g_gts"], writes=["gsb8"])
                prog.emit("tensor",
                          lambda Eo: Eo.transpose(PB[3].bitcast(BF16)[0:8, 0:128],
                                                  gsb8[:, :], idn_b[:, :]),
                          reads=["gsb8", "idn_b"], writes=["B3"])
                prog.emit("scalar",
                          lambda Eo, tc=tc: Eo.copy(grow[:, tc * 128:(tc + 1) * 128],
                                                    PB[3].bitcast(BF16)[0:8, 0:128]),
                          reads=["B3"], writes=["grow"])

    def emit_ffn_shared():
        zv1s = z_sw1.rearrange("p (c i) -> p c i", i=SHI)
        zv3s = z_sw3.rearrange("p (c i) -> p c i", i=SHI)
        for tt in range(NTT):
            ts = slice(tt * 512, tt * 512 + 512)
            for i in range(SHI // 128):
                s = i % 2
                dma("sync", wu[:, s, :, :], zv1s[:, :, i * 128:(i + 1) * 128],
                    ["zsw1"], [("wu", s)], f"wu{s}")
                dma("sync", wv3[:, s, :, :], zv3s[:, :, i * 128:(i + 1) * 128],
                    ["zsw3"], [("wv3", s)], f"wv3{s}")
                bu, bv = BK[s * 2], BK[s * 2 + 1]
                pu, pv = PB[s * 2], PB[s * 2 + 1]
                mm(pu[:, 0:512], [(wu[:, s, c, :], xh[:, c, ts]) for c in range(8)],
                   bu, [("wu", s)] + [("xh", c) for c in range(8)])
                mm(pv[:, 0:512], [(wv3[:, s, c, :], xh[:, c, ts]) for c in range(8)],
                   bv, [("wv3", s)] + [("xh", c) for c in range(8)])
                prog.emit("scalar",
                          lambda Eo, pu=pu: Eo.activation(sc0[:, :], pu[:, 0:512], AF.Silu),
                          reads=[bu], writes=["sc0"])
                prog.emit("vector",
                          lambda Eo, i=i, pv=pv: Eo.tensor_mul(hgs[:, i, :], sc0[:, :],
                                                               pv[:, 0:512]),
                          reads=["sc0", bv], writes=[("hg", i // 2)])
            emit_gating(only_tt=tt)
            for i in range(SHI // 128):
                s2 = i % 8
                dma("sync", w2s[:, s2, :], z_sw2[:, i * 1024:(i + 1) * 1024],
                    ["zsw2"], [("w2s", s2)], f"w2q{s2}")

                def fn_ysh(Eo, i=i, s2=s2):
                    ii = None
                    for o in range(8):
                        bb = 4 - 4 * (o // 4)
                        ii = Eo.matmul(PB[bb + o % 4][:, 0:512],
                                       w2s[:, s2, o * 128:(o + 1) * 128],
                                       hgs[:, i, :],
                                       start=(i == 0), stop=(i == SHI // 128 - 1))
                    return ii
                prog.emit("tensor", fn_ysh,
                          reads=[("w2s", s2), ("hg", i // 2)],
                          writes=[BK[j] for j in range(8)])
            for o in range(8):
                bb = 4 - 4 * (o // 4)
                prog.emit("vector",
                          lambda Eo, o=o, ts=ts, bb=bb: Eo.tensor_copy(
                              ya[:, o, ts], PB[bb + o % 4][:, 0:512]),
                          reads=[BK[bb + o % 4]], writes=[("ya", o)])
    def emit_ffn_experts():
        for e in range(NEXP):
            for tt in range(NTT):
                prog.emit("tensor",
                          lambda Eo, e=e, tt=tt: Eo.matmul(
                              PB[tt][:, 0:512], onehb[:, e * 128:(e + 1) * 128],
                              grow[:, tt * 512:(tt + 1) * 512], start=True, stop=True),
                          reads=["onehb", "grow"], writes=[BK[tt]])
                prog.emit("scalar",
                          lambda Eo, tt=tt: Eo.copy(g_sb[:, tt, :], PB[tt][:, 0:512]),
                          reads=[BK[tt]], writes=["g_sb"])
            zv1 = z_w1[e].rearrange("p (c i) -> p c i", i=INTER)
            zv3 = z_w3[e].rearrange("p (c i) -> p c i", i=INTER)
            for i in range(8):
                s = i % 2
                dma("sync", wu[:, s, :, :], zv1[:, :, i * 128:(i + 1) * 128],
                    [("zw1", e)], [("wu", s)], f"wu{s}")
                dma("sync", wv3[:, s, :, :], zv3[:, :, i * 128:(i + 1) * 128],
                    [("zw3", e)], [("wv3", s)], f"wv3{s}")
                if i == 1:
                    dma("sync", w2sf[:, :], z_w2[e], [("zw2", e)],
                        [("w2s", j) for j in range(8)], "w2qf")
                par = i % 2
                sA, sB = (sc0, sc1) if par == 0 else (sc2, sc3)
                kA, kB = (("sc0", "sc1") if par == 0 else ("sc2", "sc3"))
                for tt in range(NTT):
                    ts = slice(tt * 512, tt * 512 + 512)
                    pu, pv = PB[(tt % 2) * 2], PB[(tt % 2) * 2 + 1]
                    bu, bv = BK[(tt % 2) * 2], BK[(tt % 2) * 2 + 1]
                    mm(pu[:, 0:512], [(wu[:, s, c, :], xh[:, c, ts]) for c in range(8)],
                       bu, [("wu", s)] + [("xh", c) for c in range(8)])
                    mm(pv[:, 0:512], [(wv3[:, s, c, :], xh[:, c, ts]) for c in range(8)],
                       bv, [("wv3", s)] + [("xh", c) for c in range(8)])
                    prog.emit("scalar",
                              lambda Eo, pu=pu, sA=sA: Eo.activation(sA[:, :], pu[:, 0:512],
                                                                     AF.Silu),
                              reads=[bu], writes=[kA])
                    prog.emit("vector",
                              lambda Eo, tt=tt, pv=pv, sB=sB: Eo.tensor_mul(
                                  sB[:, :], pv[:, 0:512], g_sb[:, tt, :]),
                              reads=[bv, "g_sb"], writes=[kB])
                    prog.emit("vector",
                              lambda Eo, i=i, ts=ts, sA=sA, sB=sB: Eo.tensor_mul(
                                  hg[:, i, ts], sA[:, :], sB[:, :]),
                              reads=[kA, kB], writes=[("hg", i)])
            for tt in range(NTT):
                ts = slice(tt * 512, tt * 512 + 512)
                for og_i in range(2):
                    bb = 4 - 4 * og_i

                    def fn_y(Eo, ts=ts, og_i=og_i, bb=bb):
                        ii = None
                        for i in range(8):
                            for o in range(og_i * 4, og_i * 4 + 4):
                                ii = Eo.matmul(PB[bb + o % 4][:, 0:512],
                                               w2s[:, i, o * 128:(o + 1) * 128],
                                               hg[:, i, ts],
                                               start=(i == 0), stop=(i == 7))
                        return ii
                    prog.emit("tensor", fn_y,
                              reads=[("w2s", i) for i in range(8)] +
                                    [("hg", i) for i in range(8)],
                              writes=[BK[bb + j] for j in range(4)])
                    for o in range(og_i * 4, og_i * 4 + 4):
                        prog.emit("vector",
                                  lambda Eo, o=o, ts=ts, bb=bb: Eo.tensor_add(
                                      ya[:, o, ts], ya[:, o, ts],
                                      PB[bb + o % 4][:, 0:512]),
                                  reads=[BK[bb + o % 4], ("ya", o)], writes=[("ya", o)])
        for c in range(8):
            prog.emit("vector",
                      lambda Eo, c=c: Eo.scalar_tensor_tensor(xr[:, c, :], ya[:, c, :], 0.5,
                                                              xr[:, c, :], ALU.mult, ALU.add),
                      reads=[("ya", c), ("xr", c)], writes=[("xr", c)])

    def emit_attn_sub(sub):
        ALIM = cfg.get("alimit", 99)
        t0 = sub * TSUB
        ts = slice(t0, t0 + TSUB)
        zq = z_wq.rearrange("p (c j) -> p c j", j=E)
        zkv = z_wk.rearrange("p (c j) -> p c j", j=E)
        zvv = z_wv.rearrange("p (c j) -> p c j", j=E)
        zo = z_wo.rearrange("p (c j) -> p c j", j=E)
        for j in range(8):
            s = j % 2
            dma("sync", wat2[:, s, :, :], zq[:, :, j * 128:(j + 1) * 128],
                ["zwq"], [("wat2", s)], f"wat2{s}")
            pb_, bk_ = PB[j % 2], BK[j % 2]
            mm(pb_[:, 0:512], [(wat2[:, s, c, :], xh[:, c, ts]) for c in range(8)],
               bk_, [("wat2", s)] + [("xh", c) for c in range(8)])
            prog.emit("scalar",
                      lambda Eo, j=j, pb_=pb_: Eo.activation(qs[:, j, :], pb_[:, 0:512],
                                                             AF.Silu),
                      reads=[bk_], writes=[("qs", j)])
        for half in range(2 if ALIM >= 2 else 0):
            hs = slice(half * 512, half * 512 + 512)
            for c in range(8):
                dma("sync", wat[:, c, :], zkv[:, c, hs], ["zwk"], [("wat", c)],
                    f"watq{c}")

            for tc in range(4):
                tcs = slice(t0 + tc * 128, t0 + (tc + 1) * 128)
                pb_, bk_ = PB[2 + tc % 2], BK[2 + tc % 2]
                mm(pb_[:, 0:512], [(xh[:, c, tcs], wat[:, c, :]) for c in range(8)],
                   bk_, [("wat", c) for c in range(8)] + [("xh", c) for c in range(8)])
                scx = [sc0, sc1, sc2, sc3][tc]
                kx = f"sc{tc}"
                prog.emit("vector",
                          lambda Eo, pb_=pb_, half=half, scx=scx: Eo.tensor_add(
                              scx[:, :], pb_[:, 0:512], delta_bc[:, half, :]),
                          reads=[bk_, "delta_bc"], writes=[kx])
                prog.emit("scalar",
                          lambda Eo, scx=scx: Eo.activation(scx[:, :], scx[:, :],
                                                            AF.Sigmoid, scale=-1.0),
                          reads=[kx], writes=[kx])
                prog.emit("vector",
                          lambda Eo, tc=tc, hs=hs, scx=scx: Eo.tensor_copy(kt[:, tc, hs],
                                                                           scx[:, :]),
                          reads=[kx], writes=[("kt", tc)])
            for tc in range(4):
                hs2 = hs
                scx = [sc0, sc1, sc2, sc3][tc]
                kx = f"sc{tc}"
                prog.emit("scalar",
                          lambda Eo, tc=tc, hs2=hs2, scx=scx: Eo.activation(
                              gt[:, tc, hs2], scx[:, :], AF.Ln, scale=-1.0,
                              bias=cst[:, 1:2]),
                          reads=[kx, "cst"], writes=[("gt", tc)])
            for c in range(8):
                dma("sync", wat[:, c, :], zvv[:, c, hs], ["zwv"], [("wat", c)],
                    f"watq{c}")
            for tc in range(4):
                tcs = slice(t0 + tc * 128, t0 + (tc + 1) * 128)
                pb_, bk_ = PB[2 + tc % 2], BK[2 + tc % 2]
                mm(pb_[:, 0:512], [(xh[:, c, tcs], wat[:, c, :]) for c in range(8)],
                   bk_, [("wat", c) for c in range(8)] + [("xh", c) for c in range(8)])
                prog.emit("scalar",
                          lambda Eo, tc=tc, hs=hs, pb_=pb_: Eo.copy(vt[:, tc, hs],
                                                                    pb_[:, 0:512]),
                          reads=[bk_], writes=[("vt", tc)])
        for tc in range(4 if ALIM >= 3 else 0):
            for j in range(8):
                bkx = 2 + (tc * 8 + j) % 2
                prog.emit("tensor",
                          lambda Eo, tc=tc, j=j, bkx=bkx: Eo.transpose(
                              PB[bkx].bitcast(BF16)[:, 0:128],
                              kt[:, tc, j * 128:(j + 1) * 128], idn_b[:, :]),
                          reads=[("kt", tc), "idn_b"], writes=[BK[bkx]])
                prog.emit("scalar",
                          lambda Eo, tc=tc, j=j, bkx=bkx: Eo.copy(
                              kf[:, j, tc * 128:(tc + 1) * 128],
                              PB[bkx].bitcast(BF16)[:, 0:128]),
                          reads=[BK[bkx]], writes=[("kf", j)])
        SLIM = cfg.get("slimit", 99)
        for tc in range(4 if ALIM >= 4 else 0):
            for hp in range(8):
                mm(PB[2][:, hp:hp + 1],
                   [(gt[:, tc, hp * 128:(hp + 1) * 128], ones_col_b)],
                   "B2", [("gt", tc), "tri_b"])
            prog.emit("scalar", lambda Eo: Eo.activation(lam[:, :], PB[2][:, 0:8], AF.Exp),
                      reads=["B2"], writes=["lam"])
            def fA(wv_):
                h0 = wv_ * 4
                for k in range(4):
                    hp = h0 + k
                    mm(PB[0][:, k * 128:(k + 1) * 128],
                       [(gt[:, tc, hp * 128:(hp + 1) * 128], tri_b[:, :])],
                       "B0", [("gt", tc), "tri_b"])
                    mm(PB[1][:, k * 128:(k + 1) * 128],
                       [(trib_b[:, :], gt[:, tc, hp * 128:(hp + 1) * 128])],
                       "B1", [("gt", tc), "trib_b"])

            def fES(wv_):
                h0 = wv_ * 4
                prog.emit("scalar", lambda Eo: Eo.activation(ex0[:, :], PB[0][:, 0:512],
                                                             AF.Exp),
                          reads=["B0"], writes=["ex0"])
                prog.emit("scalar", lambda Eo: Eo.activation(ex1[:, :], PB[0][:, 0:512],
                                                             AF.Exp, scale=-1.0),
                          reads=["B0"], writes=["ex1"])
                prog.emit("scalar", lambda Eo: Eo.activation(ex2[:, :], PB[1][:, 0:512],
                                                             AF.Exp),
                          reads=["B1"], writes=["ex2"])
                e0v = ex0.rearrange("p (a b) -> p a b", b=128)
                e1v = ex1.rearrange("p (a b) -> p a b", b=128)
                e2v = ex2.rearrange("p (a b) -> p a b", b=128)
                prog.emit("vector",
                          lambda Eo, h0=h0, e0v=e0v: Eo.tensor_mul(
                              qtb[:, :, :], qs[:, h0:h0 + 4, tc * 128:(tc + 1) * 128], e0v),
                          reads=[("qs", h0 + k2) for k2 in range(4)] + ["ex0"],
                          writes=["qtb"])
                prog.emit("vector",
                          lambda Eo, h0=h0, e1v=e1v: Eo.tensor_mul(
                              khb[:, :, :], kf[:, h0:h0 + 4, tc * 128:(tc + 1) * 128], e1v),
                          reads=[("kf", h0 + k2) for k2 in range(4)] + ["ex1"],
                          writes=["khb"])
                ktv = kt[:, tc, h0 * 128:(h0 + 4) * 128].rearrange("p (a b) -> p a b", b=128)
                prog.emit("vector",
                          lambda Eo, ktv=ktv, e2v=e2v: Eo.tensor_mul(Ktb[:, :, :], ktv, e2v),
                          reads=[("kt", tc), "ex2"], writes=["Ktb"])
                prog.emit("vector",
                          lambda Eo, h0=h0: Eo.tensor_copy(Sbb[:, :, :], S[:, h0:h0 + 4, :]),
                          reads=[("S", h0 + k2) for k2 in range(4)], writes=["Sbb"])

            def fP(wv_):
                if SLIM < 3:
                    return
                for h2 in range(2):
                    for k in range(4):
                        pbk = 3 + h2
                        mm(PB[pbk][:, k * 128:(k + 1) * 128],
                           [(khb[h2 * 64:(h2 + 1) * 64, k, :],
                             qtb[h2 * 64:(h2 + 1) * 64, k, :])],
                           BK[pbk], ["khb", "qtb"])

            def fTail(wv_):
                if SLIM < 3:
                    return
                h0 = wv_ * 4
                obank = 5 + wv_
                for h2 in range(2 if SLIM >= 32 else 0):
                    pbk = 3 + h2
                    prog.emit("vector",
                              lambda Eo, pbk=pbk, h2=h2: Eo.tensor_mul(
                                  Pmf[:, h2 * 512:(h2 + 1) * 512],
                                  PB[pbk][:, 0:512], tri4[:, :]),
                              reads=[BK[pbk], "tri4"], writes=["Pm"])
                if SLIM < 33:
                    return
                for k in range(4):
                    hp = h0 + k
                    for h2 in range(2):
                        def fn_o(Eo, k=k, hp=hp, h2=h2, tc=tc, obank=obank):
                            sl = slice(h2 * 64, (h2 + 1) * 64)
                            tsl = slice(k * 128, (k + 1) * 128)
                            Eo.matmul(PB[obank][sl, tsl], Sbb[sl, k, :], qtb[sl, k, :],
                                      start=True, stop=False)
                            return Eo.matmul(
                                PB[obank][sl, tsl],
                                vt[:, tc, (2 * hp + h2) * 64:(2 * hp + h2 + 1) * 64],
                                Pm[:, h2 * 4 + k, :], start=False, stop=True)
                        prog.emit("tensor", fn_o,
                                  reads=["Sbb", "qtb", ("vt", tc), "Pm"],
                                  writes=[BK[obank]])
                prog.emit("scalar",
                          lambda Eo, h0=h0, obank=obank: Eo.copy(
                              of[:, h0:h0 + 4, tc * 128:(tc + 1) * 128],
                              PB[obank][:, 0:512].rearrange("p (a b) -> p a b", b=128)),
                          reads=[BK[obank]], writes=[("of", h0 + k2) for k2 in range(4)])
                if SLIM < 5:
                    return
                for k in range(4):
                    hp = h0 + k
                    for h2 in range(2):
                        mm(PB[7][h2 * 64:(h2 + 1) * 64, k * 64:(k + 1) * 64],
                           [(Ktb[:, k, h2 * 64:(h2 + 1) * 64],
                             vt[:, tc, (2 * hp + h2) * 64:(2 * hp + h2 + 1) * 64])],
                           "B7", ["Ktb", ("vt", tc)])
                for k in range(4):
                    hp = h0 + k
                    prog.emit("vector",
                              lambda Eo, k=k, hp=hp: Eo.scalar_tensor_tensor(
                                  S[:, hp, :], S[:, hp, :], lam[:, hp:hp + 1],
                                  PB[7][:, k * 64:(k + 1) * 64], ALU.mult, ALU.add),
                              reads=[("S", hp), "lam", "B7"], writes=[("S", hp)])

            if SLIM >= 2:
                fA(0)
                fES(0)
                fP(0)
                fA(1)
                fTail(0)
                fES(1)
                fP(1)
                fTail(1)
        if ALIM < 5:
            return
        mm(PB[0][0:64, 0:512], [(g1sb[:, c, :], xh[:, c, ts]) for c in range(8)],
           "B0", ["g1sb"] + [("xh", c) for c in range(8)])
        prog.emit("scalar", lambda Eo: Eo.copy(g1a[0:64, :], PB[0][0:64, 0:512]),
                  reads=["B0"], writes=["g1a"])
        for j in range(8):
            pb_, bk_ = PB[j % 4], BK[j % 4]
            scx, kx = [sc0, sc1, sc2, sc3][j % 4], f"sc{j % 4}"
            mm(pb_[:, 0:512], [(g2sb[:, j * 128:(j + 1) * 128], g1a[0:64, :])],
               bk_, ["g2sb", "g1a"])
            prog.emit("scalar", lambda Eo, pb_=pb_, scx=scx: Eo.activation(
                          scx[:, :], pb_[:, 0:512], AF.Sigmoid),
                      reads=[bk_], writes=[kx])
            prog.emit("vector",
                      lambda Eo, j=j, scx=scx: Eo.tensor_mul(og[:, j, :], of[:, j, :],
                                                             scx[:, :]),
                      reads=[("of", j), kx], writes=[("og", j)])
        sqb = kt
        for j in range(8):
            prog.emit("scalar",
                      lambda Eo, j=j: Eo.activation(sqb[:, j // 2, (j % 2) * 512:
                                                        (j % 2) * 512 + 512],
                                                    og[:, j, :], AF.Square),
                      reads=[("og", j)], writes=[("kt", j // 2)])
        def rms_mms(j):
            p = j % 2
            bqa, bqb = 2 + 2 * p, 3 + 2 * p
            sq_ap = sqb[:, j // 2, (j % 2) * 512:(j % 2) * 512 + 512]
            mm(PB[bqa][0:1, 0:512], [(ones_col_b[0:64, :], sq_ap[0:64, :])], BK[bqa],
               [("kt", j // 2), "tri_b"])
            mm(PB[bqb][0:1, 0:512], [(ones_col_b[64:128, :], sq_ap[64:128, :])], BK[bqb],
               [("kt", j // 2), "tri_b"])

        def rms_rows(j):
            p = j % 2
            bqa, bqb = 2 + 2 * p, 3 + 2 * p
            ra_j = rows[0:1, 1024:1536] if p == 0 else rows[0:1, 0:512]
            rb_j = rows[0:1, 1536:2048] if p == 0 else rows[0:1, 512:1024]
            ka = "r_a" if p == 0 else "r_m"
            kb = "r_b" if p == 0 else "r_q"
            prog.emit("scalar", lambda Eo, bqa=bqa, ra_j=ra_j: Eo.activation(
                          ra_j, PB[bqa][0:1, 0:512], AF.Sqrt,
                          bias=cst[0:1, 0:1], scale=1.0 / DH),
                      reads=[BK[bqa], "cst"], writes=[ka])
            prog.emit("scalar", lambda Eo, bqb=bqb, rb_j=rb_j: Eo.activation(
                          rb_j, PB[bqb][0:1, 0:512], AF.Sqrt,
                          bias=cst[0:1, 0:1], scale=1.0 / DH),
                      reads=[BK[bqb], "cst"], writes=[kb])
            prog.emit("vector", lambda Eo, ra_j=ra_j: Eo.reciprocal(ra_j, ra_j),
                      reads=[ka], writes=[ka])
            prog.emit("vector", lambda Eo, rb_j=rb_j: Eo.reciprocal(rb_j, rb_j),
                      reads=[kb], writes=[kb])

        def rms_bc(j):
            p = j % 2
            bbc = 6 + p
            ra_j = rows[0:1, 1024:1536] if p == 0 else rows[0:1, 0:512]
            rb_j = rows[0:1, 1536:2048] if p == 0 else rows[0:1, 512:1024]
            ka = "r_a" if p == 0 else "r_m"
            kb = "r_b" if p == 0 else "r_q"

            def fn_bc(Eo, bbc=bbc, ra_j=ra_j, rb_j=rb_j):
                Eo.matmul(PB[bbc][0:64, 0:512], ones_row_f[0:1, 0:64], ra_j,
                          start=True, stop=True)
                return Eo.matmul(PB[bbc][64:128, 0:512], ones_row_f[0:1, 0:64], rb_j,
                                 start=True, stop=True)
            prog.emit("tensor", fn_bc, reads=[ka, kb, "tri_f"], writes=[BK[bbc]])

        def rms_stt(j):
            p = j % 2
            bbc = 6 + p
            prog.emit("vector",
                      lambda Eo, j=j, bbc=bbc: Eo.scalar_tensor_tensor(
                          on[:, j, :], og[:, j, :], normw_c[:, j:j + 1],
                          PB[bbc][:, 0:512], ALU.mult, ALU.mult),
                      reads=[("og", j), "normw_c", BK[bbc]], writes=[("on", j)])

        for jp in range(4):
            j0, j1 = 2 * jp, 2 * jp + 1
            rms_mms(j0)
            rms_mms(j1)
            rms_rows(j0)
            rms_rows(j1)
            rms_bc(j0)
            rms_bc(j1)
            rms_stt(j0)
            rms_stt(j1)
        for j in range(8):
            s = j % 2
            dma("sync", wat2[:, s, :, :], zo[:, :, j * 128:(j + 1) * 128],
                ["zwo"], [("wat2", s)], f"wat2{s}")
            pb_, bk_ = PB[5 + j % 2], BK[5 + j % 2]
            mm(pb_[:, 0:512], [(wat2[:, s, c, :], on[:, c, :]) for c in range(8)],
               bk_, [("wat2", s)] + [("on", c) for c in range(8)])
            prog.emit("vector",
                      lambda Eo, j=j, ts=ts, pb_=pb_: Eo.tensor_add(xr[:, j, ts],
                                                                    xr[:, j, ts],
                                                                    pb_[:, 0:512]),
                      reads=[bk_, ("xr", j)], writes=[("xr", j)])

    LIMIT = cfg.get("limit", 99)
    for b in range(B):
        prog.emit("vector", lambda Eo: Eo.memset(S[:, :, :], 0.0),
                  reads=[], writes=[("S", hp) for hp in range(8)])
        for blk in range(NBLK):
            bs = slice(blk * TBLK, (blk + 1) * TBLK)
            xv = x_d[b].rearrange("(c p) t -> p c t", p=128)[:, :, bs]
            for c in range(8):
                dma("scalar", xr[:, c, :], xv[:, c, :],
                    [], [("xr", c)], f"xq{c}")
            if LIMIT >= 1:
                emit_ln()
            if LIMIT >= 2:
                emit_ffn_shared()
            if LIMIT >= 3:
                emit_ffn_experts()
            if LIMIT >= 4:
                emit_ln()
            if LIMIT >= 5:
                for sub in range(NSUB):
                    emit_attn_sub(sub)
            if LIMIT >= 6:
                emit_ln()
                emit_ffn_shared()
                emit_ffn_experts()
            yv = y_d[b].rearrange("(c p) t -> p c t", p=128)[:, :, bs]
            for c in range(8):
                dma("scalar", yv[:, c, :], xr[:, c, :],
                    [("xr", c)], [("ystore", b, blk, c)], f"yq{c}")

    for c in range(8):
        nc.scalar.wait_ge(prog.dsem[f"yq{c}"], prog.dcnt[f"yq{c}"])

    ctx.close()
    return nc, prog


def make_inputs(core_x, weights, cfg):
    B, NBLK, TBLK = cfg["B"], cfg["NBLK"], cfg["TBLK"]
    T = NBLK * TBLK
    xt = np.ascontiguousarray(np.transpose(np.asarray(core_x), (0, 2, 1)))
    W = {k: np.asarray(v) for k, v in weights.items()}
    d = {
        "x": xt,
        "w1T": np.ascontiguousarray(np.transpose(W["L0_ffn_w1"], (0, 2, 1))),
        "w3T": np.ascontiguousarray(np.transpose(W["L0_ffn_w3"], (0, 2, 1))),
        "w2T": np.ascontiguousarray(np.transpose(W["L0_ffn_w2"], (0, 2, 1))),
        "sw1T": np.ascontiguousarray(W["L0_ffn_sw1"].T),
        "sw3T": np.ascontiguousarray(W["L0_ffn_sw3"].T),
        "sw2T": np.ascontiguousarray(W["L0_ffn_sw2"].T),
        "gwT": np.ascontiguousarray(W["L0_ffn_gate_w"].T),
        "gbias": np.ascontiguousarray(W["L0_ffn_gate_bias"].reshape(1, NEXP)),
        "wq": np.ascontiguousarray(W["L0_attn_Wq"]),
        "wk": np.ascontiguousarray(W["L0_attn_Wk"]),
        "wv": np.ascontiguousarray(W["L0_attn_Wv"]),
        "wo": np.ascontiguousarray(W["L0_attn_Wo"]),
        "g1": np.ascontiguousarray(W["L0_attn_G1"]),
        "g2": np.ascontiguousarray(W["L0_attn_G2"]),
        "deltar": np.ascontiguousarray(W["L0_attn_delta"].reshape(8, 128)),
        "deltaf": np.ascontiguousarray(W["L0_attn_delta"].reshape(1, E)),
        "normwr": np.ascontiguousarray(W["L0_attn_norm_w"].reshape(8, 128)),
        "ctri": np.triu(np.ones((128, 128), dtype=np.float32)),
        "cidn": np.eye(128, dtype=np.float32),
        "coneh": np.kron(np.eye(NEXP), np.ones((1, 128))).astype(np.float32),
    }
    assert d["x"].shape == (B, E, T), d["x"].shape
    return d


def kernel(**inputs):
    cfg = CFG_FULL
    x_full = np.asarray(inputs["L0_attn_x"])
    n_cores = 8
    nc, _ = build(cfg)
    in_maps = []
    for c in range(n_cores):
        core_x = x_full[c * cfg["B"]:(c + 1) * cfg["B"]]
        in_maps.append(make_inputs(core_x, inputs, cfg))
    res = run_bass_kernel_spmd(nc, in_maps, core_ids=list(range(n_cores)))
    outs = []
    for c in range(n_cores):
        yc = np.asarray(res.results[c]["y"])
        outs.append(np.transpose(yc, (0, 2, 1)))
    return np.ascontiguousarray(np.concatenate(outs, axis=0), dtype=np.float32)


# version 10
# speedup vs baseline: 1.0634x; 1.0634x over previous
import bisect
import contextlib

import numpy as np

import concourse.bass as bass
import concourse.mybir as mybir
from concourse.bass_utils import run_bass_kernel_spmd

F32 = mybir.dt.float32
BF16 = mybir.dt.bfloat16
AF = mybir.ActivationFunctionType
ALU = mybir.AluOpType
AX = mybir.AxisListType

E = 1024
H = 16
DH = 64
NEXP = 8
INTER = 1024
SHI = 2048
EPS_LN = 1e-5
EPS_RMS = 1e-5

CFG_FULL = dict(B=2, NBLK=4, TBLK=1024)
TSUB = 512

ENGS = ("sync", "gpsimd", "tensor", "scalar", "vector")


class P:

    def __init__(self, nc):
        self.nc = nc
        self.eng = {"sync": nc.sync, "gpsimd": nc.gpsimd, "tensor": nc.tensor,
                    "scalar": nc.scalar, "vector": nc.vector}
        self.sem = {k: nc.alloc_semaphore(name="sem_" + k) for k in ENGS}
        self.cnt = {k: 0 for k in ENGS}
        self.dsem = {}
        self.dcnt = {}
        self.clock = {k: {j: 0 for j in ENGS} for k in ENGS}
        self.hist = {k: ([0], [dict(self.clock[k])]) for k in ENGS}
        self.last_w = {}
        self.readers = {}
        self.ivals = {}
        self._olap_cache = {}
        self.n_emit = 0
        self.n_wait = 0

    def reg(self, key, start, end):
        self.ivals[key] = (start, end)
        self._olap_cache = {}

    def _olap(self, key):
        s = self._olap_cache.get(key)
        if s is not None:
            return s
        s = []
        iv = self.ivals.get(key)
        if iv is not None:
            for k2, iv2 in self.ivals.items():
                if k2 != key and iv2[0] < iv[1] and iv[0] < iv2[1]:
                    s.append(k2)
        self._olap_cache[key] = s
        return s

    def _getdsem(self, name):
        if name not in self.dsem:
            self.dsem[name] = self.nc.alloc_semaphore(name="dq_" + name)
            self.dcnt[name] = 0
        return self.dsem[name]

    def bump(self, keys, dname):
        v = self.dcnt[dname]
        for k in keys:
            self.last_w[k] = (("D", dname), v)

    def emit(self, eng, fn, reads=(), writes=(), dma=False, dsem=None):
        waits = {}
        clk = self.clock[eng]

        def need(f, v):
            if clk.get(f, 0) < v:
                if f == "tensor" and eng == "tensor":
                    return
                if waits.get(f, 0) < v:
                    waits[f] = v

        def need_w(b):
            w = self.last_w.get(b)
            if w is not None:
                need(*w)

        def need_r(b):
            for f, v in self.readers.get(b, {}).items():
                need(f, v)

        for b in reads:
            need_w(b)
            for b2 in self._olap(b):
                need_w(b2)
        for b in writes:
            need_w(b)
            need_r(b)
            for b2 in self._olap(b):
                need_w(b2)
                need_r(b2)
        Eo = self.eng[eng]
        for f in sorted(waits, key=str):
            v = waits[f]
            if isinstance(f, tuple):
                Eo.wait_ge(self.dsem[f[1]], v)
            else:
                Eo.wait_ge(self.sem[f], v)
            self.n_wait += 1
            if clk.get(f, 0) < v:
                clk[f] = v
            if not isinstance(f, tuple):
                cnts, clocks = self.hist[f]
                i = bisect.bisect_right(cnts, v) - 1
                if i >= 0:
                    for g, gv in clocks[i].items():
                        if clk.get(g, 0) < gv:
                            clk[g] = gv
        inst = fn(Eo)
        if dma:
            assert dsem is not None
            self._getdsem(dsem)
            self.dcnt[dsem] += 16
            inst.then_inc(self.dsem[dsem], 16)
            who = ("D", dsem)
            new = self.dcnt[dsem]
        else:
            self.cnt[eng] += 1
            inst.then_inc(self.sem[eng], 1)
            who = eng
            new = self.cnt[eng]
            cnts, clocks = self.hist[eng]
            cnts.append(new)
            clocks.append(dict(clk))
        for b in writes:
            self.last_w[b] = (who, new)
            self.readers[b] = {}
        for b in reads:
            self.readers.setdefault(b, {})[eng if not dma else who] = new
        self.n_emit += 1
        return new


def build(cfg):
    B = cfg["B"]
    NBLK = cfg["NBLK"]
    TBLK = cfg["TBLK"]
    T = NBLK * TBLK
    NTT = TBLK // 512
    NSUB = TBLK // TSUB

    nc = bass.Bass("TRN2", target_bir_lowering=False, debug=False)

    x_d = nc.dram_tensor("x", [B, E, T], F32, kind="ExternalInput").ap()
    y_d = nc.dram_tensor("y", [B, E, T], F32, kind="ExternalOutput").ap()
    w1T_d = nc.dram_tensor("w1T", [NEXP, E, INTER], F32, kind="ExternalInput").ap()
    w3T_d = nc.dram_tensor("w3T", [NEXP, E, INTER], F32, kind="ExternalInput").ap()
    w2T_d = nc.dram_tensor("w2T", [NEXP, INTER, E], F32, kind="ExternalInput").ap()
    sw1T_d = nc.dram_tensor("sw1T", [E, SHI], F32, kind="ExternalInput").ap()
    sw3T_d = nc.dram_tensor("sw3T", [E, SHI], F32, kind="ExternalInput").ap()
    sw2T_d = nc.dram_tensor("sw2T", [SHI, E], F32, kind="ExternalInput").ap()
    gwT_d = nc.dram_tensor("gwT", [E, NEXP], F32, kind="ExternalInput").ap()
    gbias_d = nc.dram_tensor("gbias", [1, NEXP], F32, kind="ExternalInput").ap()
    wq_d = nc.dram_tensor("wq", [E, E], F32, kind="ExternalInput").ap()
    wk_d = nc.dram_tensor("wk", [E, E], F32, kind="ExternalInput").ap()
    wv_d = nc.dram_tensor("wv", [E, E], F32, kind="ExternalInput").ap()
    wo_d = nc.dram_tensor("wo", [E, E], F32, kind="ExternalInput").ap()
    g1_d = nc.dram_tensor("g1", [E, DH], F32, kind="ExternalInput").ap()
    g2_d = nc.dram_tensor("g2", [DH, E], F32, kind="ExternalInput").ap()
    deltar_d = nc.dram_tensor("deltar", [8, 128], F32, kind="ExternalInput").ap()
    deltaf_d = nc.dram_tensor("deltaf", [1, E], F32, kind="ExternalInput").ap()
    normwr_d = nc.dram_tensor("normwr", [8, 128], F32, kind="ExternalInput").ap()
    tri_d = nc.dram_tensor("ctri", [128, 128], F32, kind="ExternalInput").ap()
    idn_d = nc.dram_tensor("cidn", [128, 128], F32, kind="ExternalInput").ap()
    oneh_d = nc.dram_tensor("coneh", [8, NEXP * 128], F32, kind="ExternalInput").ap()

    z_w1 = nc.dram_tensor("z_w1", [NEXP, 128, E * INTER // 128], BF16).ap()
    z_w3 = nc.dram_tensor("z_w3", [NEXP, 128, E * INTER // 128], BF16).ap()
    z_w2 = nc.dram_tensor("z_w2", [NEXP, 128, INTER * E // 128], BF16).ap()
    z_sw1 = nc.dram_tensor("z_sw1", [128, E * SHI // 128], BF16).ap()
    z_sw3 = nc.dram_tensor("z_sw3", [128, E * SHI // 128], BF16).ap()
    z_sw2 = nc.dram_tensor("z_sw2", [128, SHI * E // 128], BF16).ap()
    z_wq = nc.dram_tensor("z_wq", [128, E * E // 128], BF16).ap()
    z_wk = nc.dram_tensor("z_wk", [128, E * E // 128], BF16).ap()
    z_wv = nc.dram_tensor("z_wv", [128, E * E // 128], BF16).ap()
    z_wo = nc.dram_tensor("z_wo", [128, E * E // 128], BF16).ap()

    ctx = contextlib.ExitStack()

    def sbt(name, shape, dtype):
        return ctx.enter_context(nc.sbuf_tensor(name, shape, dtype)).ap()

    xr = sbt("xr", [128, 8, TBLK], F32)
    xh = sbt("xh", [128, 8, TBLK], BF16)
    S = sbt("S", [128, 8, DH], F32)
    tri_f = sbt("tri_f", [128, 128], F32)
    idn_f = sbt("idn_f", [128, 128], F32)
    tri_b = sbt("tri_b", [128, 128], BF16)
    trib_b = sbt("trib_b", [128, 128], BF16)
    idn_b = sbt("idn_b", [128, 128], BF16)
    delta_c = sbt("delta_c", [128, 8], F32)
    ndelta_c = sbt("ndelta_c", [128, 8], F32)
    normw_c = sbt("normw_c", [128, 8], F32)
    bias_bc = sbt("bias_bc", [128, 8], F32)
    delta_bc = sbt("delta_bc", [128, 2, 512], F32)
    g1sb = sbt("g1sb", [128, 8, DH], BF16)
    g2sb = sbt("g2sb", [DH, E], BF16)
    gwsb = sbt("gwsb", [128, 8, 8], BF16)
    onehb = sbt("onehb", [8, NEXP * 128], BF16)
    grow = sbt("grow", [8, 8 * 128], BF16)
    rows = sbt("rows", [1, 2048], F32)
    srows = sbt("srows", [1, 1032], F32)
    gsm = sbt("gsm", [128, 64], F32)
    gsb8 = sbt("gsb8", [128, 8], BF16)
    lam = sbt("lam", [128, 8], F32)
    sc0 = sbt("sc0", [128, 512], F32)
    sc1 = sbt("sc1", [128, 512], F32)
    sc2 = sbt("sc2", [128, 512], F32)
    sc3 = sbt("sc3", [128, 512], F32)
    cst = sbt("cst", [128, 2], F32)
    tri4 = sbt("tri4", [128, 512], F32)
    arena = ctx.enter_context(nc.sbuf_tensor("arena", [128, 78336], mybir.dt.uint8)).ap()

    ones_col_f = tri_f[:, 127:128]
    ones_row_f = tri_f[0:1, :]
    ones_col_b = tri_b[:, 127:128]

    prog = P(nc)

    def carve(off, shape, dtype, key=None, subkeys=None):
        n = 1
        for s in shape:
            n *= s
        esz = 2 if dtype == BF16 else 4
        nbytes = n * esz
        v = arena[:, off:off + nbytes].bitcast(dtype)
        if len(shape) > 1:
            spec = "p (" + " ".join(f"a{i}" for i in range(len(shape))) + ") -> p " + \
                   " ".join(f"a{i}" for i in range(len(shape)))
            kw = {f"a{i}": s for i, s in enumerate(shape)}
            v = v.rearrange(spec, **kw)
        if key is not None:
            prog.reg(key, off, off + nbytes)
        if subkeys is not None:
            prefix, nslice = subkeys
            sl = nbytes // nslice
            for i in range(nslice):
                prog.reg((prefix, i), off + i * sl, off + (i + 1) * sl)
        return v

    hg = carve(0, [8, 1024], BF16, subkeys=("hg", 8))
    hgs = carve(0, [16, 512], BF16)
    YA0 = 16384
    ya = carve(YA0, [8, TBLK], F32, subkeys=("ya", 8))
    W2S0 = YA0 + TBLK * 32
    w2s = carve(W2S0, [8, 1024], BF16, subkeys=("w2s", 8))
    w2sf = carve(W2S0, [8192], BF16)
    WOFF = W2S0 + 16384
    wu = carve(WOFF, [2, 8, 128], BF16, subkeys=("wu", 2))
    wv3 = carve(WOFF + 4096, [2, 8, 128], BF16, subkeys=("wv3", 2))
    g_sb = carve(49152 + 16384 + 8192, [2, 512], F32, key="g_sb")
    qs = carve(0, [8, TSUB], BF16, subkeys=("qs", 8))
    kf = carve(8192, [8, TSUB], BF16, subkeys=("kf", 8))
    kt = carve(16384, [4, E], BF16, subkeys=("kt", 4))
    vt = carve(24576, [4, E], BF16, subkeys=("vt", 4))
    gt = carve(32768, [4, E], BF16, subkeys=("gt", 4))
    of = carve(40960, [8, TSUB], BF16, subkeys=("of", 8))
    ex0 = carve(49152, [512], F32, key="ex0")
    ex1 = carve(51200, [512], F32, key="ex1")
    ex2 = carve(53248, [512], F32, key="ex2")
    qtb = carve(55296, [4, 128], BF16, key="qtb")
    khb = carve(56320, [4, 128], BF16, key="khb")
    Ktb = carve(57344, [4, 128], BF16, key="Ktb")
    Sbb = carve(58368, [4, DH], BF16, key="Sbb")
    Pm = carve(58880, [8, 128], BF16, key="Pm")
    Pmf = carve(58880, [1024], BF16, key="Pm")
    wat = carve(60928, [8, 512], BF16, subkeys=("wat", 8))
    wat2 = carve(69120, [2, 8, 128], BF16, subkeys=("wat2", 2))
    og = carve(0, [8, TSUB], BF16, subkeys=("og", 8))
    on = carve(8192, [8, TSUB], BF16, subkeys=("on", 8))
    g1a = carve(73216, [512], BF16, key="g1a")
    stg = [carve(0, [16384], BF16, key="stg0"),
           carve(32768, [16384], BF16, key="stg1")]

    PB = [ctx.enter_context(nc.psum_tensor(f"pb{i}", [128, 512], F32)).ap()
          for i in range(8)]
    BK = [f"B{i}" for i in range(8)]

    def mm(bank_ap, pairs, bkey, reads, start0=True, stopN=True):
        def fn(Eo):
            n = len(pairs)
            ii = None
            for i, (lhsT, rhs) in enumerate(pairs):
                ii = Eo.matmul(bank_ap, lhsT, rhs,
                               start=(i == 0 and start0), stop=(i == n - 1 and stopN))
            return ii
        return prog.emit("tensor", fn, reads=reads, writes=[bkey])

    def dma(eng, out, in_, reads, writes, dsem):
        prog.emit(eng, lambda Eo: Eo.dma_start(out=out, in_=in_),
                  reads=reads, writes=writes, dma=True, dsem=dsem)

    prog.emit("vector", lambda Eo: Eo.memset(cst[:, 0:1], EPS_LN),
              reads=[], writes=["cst"])
    prog.emit("vector", lambda Eo: Eo.memset(cst[:, 1:2], 1.0),
              reads=[], writes=["cst"])
    dma("sync", tri_f[:, :], tri_d, [], ["tri_f"], "setup")
    dma("sync", idn_f[:, :], idn_d, [], ["idn_f"], "setup")
    dma("sync", sc0[0:8, 0:128], deltar_d, [], ["sc0"], "setup")
    dma("sync", sc1[0:8, 0:128], normwr_d, [], ["sc1"], "setup")
    dma("sync", srows[0:1, 0:8], gbias_d, [], ["sr_gb"], "setup")
    dma("sync", srows[0:1, 8:8 + E], deltaf_d, [], ["sr_df"], "setup")
    prog.bump(["tri_f", "idn_f", "sc0", "sc1", "sr_gb", "sr_df"], "setup")
    for k4 in range(4):
        prog.emit("scalar", lambda Eo, k4=k4: Eo.copy(tri4[:, k4 * 128:(k4 + 1) * 128],
                                                      tri_f[:, :]),
                  reads=["tri_f"], writes=["tri4"])
    prog.emit("vector", lambda Eo: Eo.tensor_copy(tri_b[:, :], tri_f[:, :]),
              reads=["tri_f"], writes=["tri_b"])
    prog.emit("vector", lambda Eo: Eo.tensor_scalar(trib_b[:, :], tri_f[:, :], 1.0, -1.0,
                                                    ALU.subtract, ALU.mult),
              reads=["tri_f"], writes=["trib_b"])
    prog.emit("vector", lambda Eo: Eo.tensor_copy(idn_b[:, :], idn_f[:, :]),
              reads=["idn_f"], writes=["idn_b"])

    prog.emit("tensor", lambda Eo: Eo.transpose(PB[0][:, 0:8], sc0[0:8, 0:128],
                                                idn_f[0:8, 0:8]),
              reads=["sc0", "idn_f"], writes=["B0"])
    prog.emit("scalar", lambda Eo: Eo.copy(delta_c[:, :], PB[0][:, 0:8]),
              reads=["B0"], writes=["delta_c"])
    prog.emit("tensor", lambda Eo: Eo.transpose(PB[1][:, 0:8], sc1[0:8, 0:128],
                                                idn_f[0:8, 0:8]),
              reads=["sc1", "idn_f"], writes=["B1"])
    prog.emit("scalar", lambda Eo: Eo.copy(normw_c[:, :], PB[1][:, 0:8]),
              reads=["B1"], writes=["normw_c"])
    prog.emit("vector", lambda Eo: Eo.tensor_scalar_mul(ndelta_c[:, :], delta_c[:, :], -1.0),
              reads=["delta_c"], writes=["ndelta_c"])

    prog.emit("tensor", lambda Eo: Eo.matmul(PB[2][:, 0:8], ones_row_f, srows[0:1, 0:8],
                                             start=True, stop=True),
              reads=["sr_gb", "tri_f"], writes=["B2"])
    prog.emit("scalar", lambda Eo: Eo.copy(bias_bc[:, :], PB[2][:, 0:8]),
              reads=["B2"], writes=["bias_bc"])
    for hf in range(2):
        prog.emit("tensor",
                  lambda Eo, hf=hf: Eo.matmul(PB[3][:, 0:512], ones_row_f,
                                              srows[0:1, 8 + hf * 512:8 + (hf + 1) * 512],
                                              start=True, stop=True),
                  reads=["sr_df", "tri_f"], writes=["B3"])
        prog.emit("scalar", lambda Eo, hf=hf: Eo.copy(delta_bc[:, hf, :], PB[3][:, 0:512]),
                  reads=["B3"], writes=["delta_bc"])

    dma("gpsimd", g1sb[:, :, :], g1_d.rearrange("(c p) j -> p c j", p=128),
        [], ["g1sb"], "smallw")
    dma("gpsimd", g2sb[:, :], g2_d, [], ["g2sb"], "smallw")
    dma("gpsimd", gwsb[:, :, :], gwT_d.rearrange("(c p) e -> p c e", p=128),
        [], ["gwsb"], "smallw")
    dma("gpsimd", onehb[:, :], oneh_d, [], ["onehb"], "smallw")
    prog.bump(["g1sb", "g2sb", "gwsb", "onehb"], "smallw")

    conv_list = [
        (sw1T_d.rearrange("(c p) i -> p c i", p=128), z_sw1, "zsw1"),
        (sw3T_d.rearrange("(c p) i -> p c i", p=128), z_sw3, "zsw3"),
        (sw2T_d.rearrange("(ic p) o -> p ic o", p=128), z_sw2, "zsw2"),
    ]
    for e in range(NEXP):
        conv_list.append((w1T_d[e].rearrange("(c p) i -> p c i", p=128),
                          z_w1[e], ("zw1", e)))
        conv_list.append((w3T_d[e].rearrange("(c p) i -> p c i", p=128),
                          z_w3[e], ("zw3", e)))
        conv_list.append((w2T_d[e].rearrange("(ic p) o -> p ic o", p=128),
                          z_w2[e], ("zw2", e)))
    conv_list += [
        (wq_d.rearrange("(c p) j -> p c j", p=128), z_wq, "zwq"),
        (wk_d.rearrange("(c p) j -> p c j", p=128), z_wk, "zwk"),
        (wv_d.rearrange("(c p) j -> p c j", p=128), z_wv, "zwv"),
        (wo_d.rearrange("(c p) j -> p c j", p=128), z_wo, "zwo"),
    ]
    zslot = {}
    ci = 0
    for src_, dst, zkey in conv_list:
        a, bdim = src_.shape[1], src_.shape[2]
        n = a * bdim
        assert n <= 16384, (a, bdim)
        s = ci % 2
        stv = stg[s][:, 0:n].rearrange("p (a b) -> p a b", a=a)
        dma("gpsimd", stv, src_, [], [f"stg{s}"], f"stgq{s}")
        dma("gpsimd", dst[:, 0:n], stg[s][:, 0:n], [f"stg{s}"], [zkey], f"zst{s}")
        zslot[zkey] = f"zst{s}"
        ci += 1

    def emit_ln():
        def stats(tt):
            ts = slice(tt * 512, tt * 512 + 512)
            b0, b1 = (0, 1) if tt == 0 else (4, 5)
            for c in range(8):
                prog.emit("scalar",
                          lambda Eo, c=c, ts=ts: Eo.activation(xh[:, c, ts], xr[:, c, ts],
                                                               AF.Square),
                          reads=[("xr", c)], writes=[("xh", c)])
            mm(PB[b0][0:1, 0:512], [(ones_col_f, xr[:, c, ts]) for c in range(8)],
               BK[b0], [("xr", c) for c in range(8)] + ["tri_f"])
            mm(PB[b1][0:1, 0:512], [(ones_col_b, xh[:, c, ts]) for c in range(8)],
               BK[b1], [("xh", c) for c in range(8)] + ["tri_b"])

        r_m = rows[0:1, 0:512]
        r_q = rows[0:1, 512:1024]
        r_a = rows[0:1, 1024:1536]
        r_b = rows[0:1, 1536:2048]

        def rows_f(tt):
            b0, b1 = (0, 1) if tt == 0 else (4, 5)
            prog.emit("scalar", lambda Eo, b0=b0: Eo.mul(r_m, PB[b0][0:1, 0:512], 1.0 / E),
                      reads=[BK[b0]], writes=["r_m"])
            prog.emit("scalar", lambda Eo, b1=b1: Eo.mul(r_q, PB[b1][0:1, 0:512], 1.0 / E),
                      reads=[BK[b1]], writes=["r_q"])
            prog.emit("vector", lambda Eo: Eo.scalar_tensor_tensor(r_a, r_m, 1.0, r_m,
                                                                   ALU.mult, ALU.mult),
                      reads=["r_m"], writes=["r_a"])
            prog.emit("vector", lambda Eo: Eo.tensor_sub(r_q, r_q, r_a),
                      reads=["r_q", "r_a"], writes=["r_q"])
            prog.emit("scalar", lambda Eo: Eo.activation(r_a, r_q, AF.Sqrt,
                                                         bias=cst[0:1, 0:1]),
                      reads=["r_q", "cst"], writes=["r_a"])
            prog.emit("vector", lambda Eo: Eo.reciprocal(r_a, r_a),
                      reads=["r_a"], writes=["r_a"])
            prog.emit("vector", lambda Eo: Eo.scalar_tensor_tensor(r_b, r_m, -1.0, r_a,
                                                                   ALU.mult, ALU.mult),
                      reads=["r_m", "r_a"], writes=["r_b"])

        def bcast(tt):
            b2, b3 = (2, 3) if tt == 0 else (6, 7)
            prog.emit("tensor", lambda Eo, b2=b2: Eo.matmul(PB[b2][:, 0:512], ones_row_f,
                                                            r_a, start=True, stop=True),
                      reads=["r_a", "tri_f"], writes=[BK[b2]])
            prog.emit("tensor", lambda Eo, b3=b3: Eo.matmul(PB[b3][:, 0:512], ones_row_f,
                                                            r_b, start=True, stop=True),
                      reads=["r_b", "tri_f"], writes=[BK[b3]])

        def apply(tt):
            ts = slice(tt * 512, tt * 512 + 512)
            b2, b3 = (2, 3) if tt == 0 else (6, 7)
            scx = sc0 if tt == 0 else sc1
            kx = "sc0" if tt == 0 else "sc1"
            for c in range(8):
                prog.emit("vector",
                          lambda Eo, c=c, ts=ts, b2=b2, scx=scx: Eo.tensor_mul(
                              scx[:, :], xr[:, c, ts], PB[b2][:, 0:512]),
                          reads=[("xr", c), BK[b2]], writes=[kx])
                prog.emit("vector",
                          lambda Eo, c=c, ts=ts, b3=b3, scx=scx: Eo.tensor_add(
                              xh[:, c, ts], scx[:, :], PB[b3][:, 0:512]),
                          reads=[kx, BK[b3]], writes=[("xh", c)])

        stats(0)
        if NTT > 1:
            stats(1)
        rows_f(0)
        bcast(0)
        if NTT > 1:
            rows_f(1)
            bcast(1)
        apply(0)
        if NTT > 1:
            apply(1)

    def emit_gating(only_tt=None, stage=None):
        st = gsm[:, 0:8]
        mx = gsm[:, 8:9]
        sm = gsm[:, 9:10]
        stb = gsm[:, 16:24]
        mx8 = gsm[:, 24:32]
        m2 = gsm[:, 25:26]
        msk = gsm[:, 32:40]
        gts = gsm[:, 40:48]
        for tt in ([only_tt] if only_tt is not None else range(NTT)):
            ts = slice(tt * 512, tt * 512 + 512)
            if stage in (None, 1):
                mm(PB[2][0:8, 0:512], [(gwsb[:, c, :], xh[:, c, ts]) for c in range(8)],
                   "B2", [("xh", c) for c in range(8)] + ["gwsb"])
                prog.emit("scalar", lambda Eo: Eo.copy(sc0[0:8, 0:512], PB[2][0:8, 0:512]),
                          reads=["B2"], writes=["sc0"])
            if stage == 1:
                continue
            for q in range(4):
                tc = tt * 4 + q
                prog.emit("tensor",
                          lambda Eo, q=q: Eo.transpose(PB[3][:, 0:8],
                                                       sc0[0:8, q * 128:(q + 1) * 128],
                                                       idn_f[0:8, 0:8]),
                          reads=["sc0", "idn_f"], writes=["B3"])
                prog.emit("scalar", lambda Eo: Eo.copy(st, PB[3][:, 0:8]),
                          reads=["B3"], writes=["g_st"])
                prog.emit("vector", lambda Eo: Eo.reduce_max(mx, st, AX.X),
                          reads=["g_st"], writes=["g_mx"])
                prog.emit("vector", lambda Eo: Eo.tensor_scalar(st, st, mx, None, ALU.subtract),
                          reads=["g_st", "g_mx"], writes=["g_st"])
                prog.emit("scalar", lambda Eo: Eo.activation(st, st, AF.Exp),
                          reads=["g_st"], writes=["g_st"])
                prog.emit("vector", lambda Eo: Eo.reduce_sum(sm, st, AX.X),
                          reads=["g_st"], writes=["g_sm"])
                prog.emit("vector", lambda Eo: Eo.reciprocal(sm, sm),
                          reads=["g_sm"], writes=["g_sm"])
                prog.emit("vector", lambda Eo: Eo.tensor_scalar(st, st, sm, None, ALU.mult),
                          reads=["g_st", "g_sm"], writes=["g_st"])
                prog.emit("vector", lambda Eo: Eo.tensor_add(stb, st, bias_bc[:, :]),
                          reads=["g_st", "bias_bc"], writes=["g_stb"])
                prog.emit("vector", lambda Eo: Eo.max(out=mx8, in_=stb),
                          reads=["g_stb"], writes=["g_mx8"])
                prog.emit("vector", lambda Eo: Eo.tensor_scalar(msk, stb, m2, None, ALU.is_ge),
                          reads=["g_stb", "g_mx8"], writes=["g_msk"])
                prog.emit("vector", lambda Eo: Eo.tensor_mul(gts, st, msk),
                          reads=["g_st", "g_msk"], writes=["g_gts"])
                prog.emit("vector", lambda Eo: Eo.tensor_copy(gsb8[:, :], gts),
                          reads=["g_gts"], writes=["gsb8"])
                prog.emit("tensor",
                          lambda Eo: Eo.transpose(PB[3].bitcast(BF16)[0:8, 0:128],
                                                  gsb8[:, :], idn_b[:, :]),
                          reads=["gsb8", "idn_b"], writes=["B3"])
                prog.emit("scalar",
                          lambda Eo, tc=tc: Eo.copy(grow[:, tc * 128:(tc + 1) * 128],
                                                    PB[3].bitcast(BF16)[0:8, 0:128]),
                          reads=["B3"], writes=["grow"])

    def emit_ffn_shared():
        zv1s = z_sw1.rearrange("p (c i) -> p c i", i=SHI)
        zv3s = z_sw3.rearrange("p (c i) -> p c i", i=SHI)
        for tt in range(NTT):
            ts = slice(tt * 512, tt * 512 + 512)
            for i in range(SHI // 128):
                s = i % 2
                dma("sync", wu[:, s, :, :], zv1s[:, :, i * 128:(i + 1) * 128],
                    ["zsw1"], [("wu", s)], f"wu{s}")
                dma("sync", wv3[:, s, :, :], zv3s[:, :, i * 128:(i + 1) * 128],
                    ["zsw3"], [("wv3", s)], f"wv3{s}")
                bu, bv = BK[s * 2], BK[s * 2 + 1]
                pu, pv = PB[s * 2], PB[s * 2 + 1]
                mm(pu[:, 0:512], [(wu[:, s, c, :], xh[:, c, ts]) for c in range(8)],
                   bu, [("wu", s)] + [("xh", c) for c in range(8)])
                mm(pv[:, 0:512], [(wv3[:, s, c, :], xh[:, c, ts]) for c in range(8)],
                   bv, [("wv3", s)] + [("xh", c) for c in range(8)])
                prog.emit("scalar",
                          lambda Eo, pu=pu: Eo.activation(sc0[:, :], pu[:, 0:512], AF.Silu),
                          reads=[bu], writes=["sc0"])
                prog.emit("vector",
                          lambda Eo, i=i, pv=pv: Eo.tensor_mul(hgs[:, i, :], sc0[:, :],
                                                               pv[:, 0:512]),
                          reads=["sc0", bv], writes=[("hg", i // 2)])
            emit_gating(only_tt=tt, stage=1)
            for i in range(SHI // 128):
                s2 = i % 8
                dma("sync", w2s[:, s2, :], z_sw2[:, i * 1024:(i + 1) * 1024],
                    ["zsw2"], [("w2s", s2)], f"w2q{s2}")

                def fn_ysh(Eo, i=i, s2=s2):
                    ii = None
                    for o in range(8):
                        bb = 4 - 4 * (o // 4)
                        ii = Eo.matmul(PB[bb + o % 4][:, 0:512],
                                       w2s[:, s2, o * 128:(o + 1) * 128],
                                       hgs[:, i, :],
                                       start=(i == 0), stop=(i == SHI // 128 - 1))
                    return ii
                prog.emit("tensor", fn_ysh,
                          reads=[("w2s", s2), ("hg", i // 2)],
                          writes=[BK[j] for j in range(8)])
            for o in range(8):
                bb = 4 - 4 * (o // 4)
                prog.emit("vector",
                          lambda Eo, o=o, ts=ts, bb=bb: Eo.tensor_copy(
                              ya[:, o, ts], PB[bb + o % 4][:, 0:512]),
                          reads=[BK[bb + o % 4]], writes=[("ya", o)])
            emit_gating(only_tt=tt, stage=2)
    def emit_ffn_experts():
        for e in range(NEXP):
            for tt in range(NTT):
                prog.emit("tensor",
                          lambda Eo, e=e, tt=tt: Eo.matmul(
                              PB[tt][:, 0:512], onehb[:, e * 128:(e + 1) * 128],
                              grow[:, tt * 512:(tt + 1) * 512], start=True, stop=True),
                          reads=["onehb", "grow"], writes=[BK[tt]])
                prog.emit("scalar",
                          lambda Eo, tt=tt: Eo.copy(g_sb[:, tt, :], PB[tt][:, 0:512]),
                          reads=[BK[tt]], writes=["g_sb"])
            zv1 = z_w1[e].rearrange("p (c i) -> p c i", i=INTER)
            zv3 = z_w3[e].rearrange("p (c i) -> p c i", i=INTER)
            for i in range(8):
                s = i % 2
                dma("sync", wu[:, s, :, :], zv1[:, :, i * 128:(i + 1) * 128],
                    [("zw1", e)], [("wu", s)], f"wu{s}")
                dma("sync", wv3[:, s, :, :], zv3[:, :, i * 128:(i + 1) * 128],
                    [("zw3", e)], [("wv3", s)], f"wv3{s}")
                if i == 1:
                    dma("sync", w2sf[:, :], z_w2[e], [("zw2", e)],
                        [("w2s", j) for j in range(8)], "w2qf")
                par = i % 2
                sA, sB = (sc0, sc1) if par == 0 else (sc2, sc3)
                kA, kB = (("sc0", "sc1") if par == 0 else ("sc2", "sc3"))
                for tt in range(NTT):
                    ts = slice(tt * 512, tt * 512 + 512)
                    pu, pv = PB[(tt % 2) * 2], PB[(tt % 2) * 2 + 1]
                    bu, bv = BK[(tt % 2) * 2], BK[(tt % 2) * 2 + 1]
                    mm(pu[:, 0:512], [(wu[:, s, c, :], xh[:, c, ts]) for c in range(8)],
                       bu, [("wu", s)] + [("xh", c) for c in range(8)])
                    mm(pv[:, 0:512], [(wv3[:, s, c, :], xh[:, c, ts]) for c in range(8)],
                       bv, [("wv3", s)] + [("xh", c) for c in range(8)])
                    prog.emit("scalar",
                              lambda Eo, pu=pu, sA=sA: Eo.activation(sA[:, :], pu[:, 0:512],
                                                                     AF.Silu),
                              reads=[bu], writes=[kA])
                    prog.emit("vector",
                              lambda Eo, tt=tt, pv=pv, sB=sB: Eo.tensor_mul(
                                  sB[:, :], pv[:, 0:512], g_sb[:, tt, :]),
                              reads=[bv, "g_sb"], writes=[kB])
                    prog.emit("vector",
                              lambda Eo, i=i, ts=ts, sA=sA, sB=sB: Eo.tensor_mul(
                                  hg[:, i, ts], sA[:, :], sB[:, :]),
                              reads=[kA, kB], writes=[("hg", i)])
            for tt in range(NTT):
                ts = slice(tt * 512, tt * 512 + 512)
                for og_i in range(2):
                    bb = 4 - 4 * og_i

                    def fn_y(Eo, ts=ts, og_i=og_i, bb=bb):
                        ii = None
                        for i in range(8):
                            for o in range(og_i * 4, og_i * 4 + 4):
                                ii = Eo.matmul(PB[bb + o % 4][:, 0:512],
                                               w2s[:, i, o * 128:(o + 1) * 128],
                                               hg[:, i, ts],
                                               start=(i == 0), stop=(i == 7))
                        return ii
                    prog.emit("tensor", fn_y,
                              reads=[("w2s", i) for i in range(8)] +
                                    [("hg", i) for i in range(8)],
                              writes=[BK[bb + j] for j in range(4)])
                    for o in range(og_i * 4, og_i * 4 + 4):
                        prog.emit("vector",
                                  lambda Eo, o=o, ts=ts, bb=bb: Eo.tensor_add(
                                      ya[:, o, ts], ya[:, o, ts],
                                      PB[bb + o % 4][:, 0:512]),
                                  reads=[BK[bb + o % 4], ("ya", o)], writes=[("ya", o)])
        for c in range(8):
            prog.emit("vector",
                      lambda Eo, c=c: Eo.scalar_tensor_tensor(xr[:, c, :], ya[:, c, :], 0.5,
                                                              xr[:, c, :], ALU.mult, ALU.add),
                      reads=[("ya", c), ("xr", c)], writes=[("xr", c)])

    def emit_attn_sub(sub):
        ALIM = cfg.get("alimit", 99)
        t0 = sub * TSUB
        ts = slice(t0, t0 + TSUB)
        zq = z_wq.rearrange("p (c j) -> p c j", j=E)
        zkv = z_wk.rearrange("p (c j) -> p c j", j=E)
        zvv = z_wv.rearrange("p (c j) -> p c j", j=E)
        zo = z_wo.rearrange("p (c j) -> p c j", j=E)
        for j in range(8):
            s = j % 2
            dma("sync", wat2[:, s, :, :], zq[:, :, j * 128:(j + 1) * 128],
                ["zwq"], [("wat2", s)], f"wat2{s}")
            pb_, bk_ = PB[j % 2], BK[j % 2]
            mm(pb_[:, 0:512], [(wat2[:, s, c, :], xh[:, c, ts]) for c in range(8)],
               bk_, [("wat2", s)] + [("xh", c) for c in range(8)])
            prog.emit("scalar",
                      lambda Eo, j=j, pb_=pb_: Eo.activation(qs[:, j, :], pb_[:, 0:512],
                                                             AF.Silu),
                      reads=[bk_], writes=[("qs", j)])
        for half in range(2 if ALIM >= 2 else 0):
            hs = slice(half * 512, half * 512 + 512)
            for c in range(8):
                dma("sync", wat[:, c, :], zkv[:, c, hs], ["zwk"], [("wat", c)],
                    f"watq{c}")

            for tc in range(4):
                tcs = slice(t0 + tc * 128, t0 + (tc + 1) * 128)
                pb_, bk_ = PB[2 + tc % 2], BK[2 + tc % 2]
                mm(pb_[:, 0:512], [(xh[:, c, tcs], wat[:, c, :]) for c in range(8)],
                   bk_, [("wat", c) for c in range(8)] + [("xh", c) for c in range(8)])
                scx = [sc0, sc1, sc2, sc3][tc]
                kx = f"sc{tc}"
                prog.emit("vector",
                          lambda Eo, pb_=pb_, half=half, scx=scx: Eo.tensor_add(
                              scx[:, :], pb_[:, 0:512], delta_bc[:, half, :]),
                          reads=[bk_, "delta_bc"], writes=[kx])
                prog.emit("scalar",
                          lambda Eo, scx=scx: Eo.activation(scx[:, :], scx[:, :],
                                                            AF.Sigmoid, scale=-1.0),
                          reads=[kx], writes=[kx])
                prog.emit("vector",
                          lambda Eo, tc=tc, hs=hs, scx=scx: Eo.tensor_copy(kt[:, tc, hs],
                                                                           scx[:, :]),
                          reads=[kx], writes=[("kt", tc)])
            for tc in range(4):
                hs2 = hs
                scx = [sc0, sc1, sc2, sc3][tc]
                kx = f"sc{tc}"
                prog.emit("scalar",
                          lambda Eo, tc=tc, hs2=hs2, scx=scx: Eo.activation(
                              gt[:, tc, hs2], scx[:, :], AF.Ln, scale=-1.0,
                              bias=cst[:, 1:2]),
                          reads=[kx, "cst"], writes=[("gt", tc)])
            for c in range(8):
                dma("sync", wat[:, c, :], zvv[:, c, hs], ["zwv"], [("wat", c)],
                    f"watq{c}")
            for tc in range(4):
                tcs = slice(t0 + tc * 128, t0 + (tc + 1) * 128)
                pb_, bk_ = PB[2 + tc % 2], BK[2 + tc % 2]
                mm(pb_[:, 0:512], [(xh[:, c, tcs], wat[:, c, :]) for c in range(8)],
                   bk_, [("wat", c) for c in range(8)] + [("xh", c) for c in range(8)])
                prog.emit("scalar",
                          lambda Eo, tc=tc, hs=hs, pb_=pb_: Eo.copy(vt[:, tc, hs],
                                                                    pb_[:, 0:512]),
                          reads=[bk_], writes=[("vt", tc)])
        for tc in range(4 if ALIM >= 3 else 0):
            for j in range(8):
                bkx = 2 + (tc * 8 + j) % 2
                prog.emit("tensor",
                          lambda Eo, tc=tc, j=j, bkx=bkx: Eo.transpose(
                              PB[bkx].bitcast(BF16)[:, 0:128],
                              kt[:, tc, j * 128:(j + 1) * 128], idn_b[:, :]),
                          reads=[("kt", tc), "idn_b"], writes=[BK[bkx]])
                prog.emit("scalar",
                          lambda Eo, tc=tc, j=j, bkx=bkx: Eo.copy(
                              kf[:, j, tc * 128:(tc + 1) * 128],
                              PB[bkx].bitcast(BF16)[:, 0:128]),
                          reads=[BK[bkx]], writes=[("kf", j)])
        SLIM = cfg.get("slimit", 99)
        for tc in range(4 if ALIM >= 4 else 0):
            for hp in range(8):
                mm(PB[2][:, hp:hp + 1],
                   [(gt[:, tc, hp * 128:(hp + 1) * 128], ones_col_b)],
                   "B2", [("gt", tc), "tri_b"])
            prog.emit("scalar", lambda Eo: Eo.activation(lam[:, :], PB[2][:, 0:8], AF.Exp),
                      reads=["B2"], writes=["lam"])
            def fA(wv_):
                h0 = wv_ * 4
                for k in range(4):
                    hp = h0 + k
                    mm(PB[0][:, k * 128:(k + 1) * 128],
                       [(gt[:, tc, hp * 128:(hp + 1) * 128], tri_b[:, :])],
                       "B0", [("gt", tc), "tri_b"])
                    mm(PB[1][:, k * 128:(k + 1) * 128],
                       [(trib_b[:, :], gt[:, tc, hp * 128:(hp + 1) * 128])],
                       "B1", [("gt", tc), "trib_b"])

            def fES(wv_):
                h0 = wv_ * 4
                prog.emit("scalar", lambda Eo: Eo.activation(ex0[:, :], PB[0][:, 0:512],
                                                             AF.Exp),
                          reads=["B0"], writes=["ex0"])
                prog.emit("scalar", lambda Eo: Eo.activation(ex1[:, :], PB[0][:, 0:512],
                                                             AF.Exp, scale=-1.0),
                          reads=["B0"], writes=["ex1"])
                prog.emit("scalar", lambda Eo: Eo.activation(ex2[:, :], PB[1][:, 0:512],
                                                             AF.Exp),
                          reads=["B1"], writes=["ex2"])
                e0v = ex0.rearrange("p (a b) -> p a b", b=128)
                e1v = ex1.rearrange("p (a b) -> p a b", b=128)
                e2v = ex2.rearrange("p (a b) -> p a b", b=128)
                prog.emit("vector",
                          lambda Eo, h0=h0, e0v=e0v: Eo.tensor_mul(
                              qtb[:, :, :], qs[:, h0:h0 + 4, tc * 128:(tc + 1) * 128], e0v),
                          reads=[("qs", h0 + k2) for k2 in range(4)] + ["ex0"],
                          writes=["qtb"])
                prog.emit("vector",
                          lambda Eo, h0=h0, e1v=e1v: Eo.tensor_mul(
                              khb[:, :, :], kf[:, h0:h0 + 4, tc * 128:(tc + 1) * 128], e1v),
                          reads=[("kf", h0 + k2) for k2 in range(4)] + ["ex1"],
                          writes=["khb"])
                ktv = kt[:, tc, h0 * 128:(h0 + 4) * 128].rearrange("p (a b) -> p a b", b=128)
                prog.emit("vector",
                          lambda Eo, ktv=ktv, e2v=e2v: Eo.tensor_mul(Ktb[:, :, :], ktv, e2v),
                          reads=[("kt", tc), "ex2"], writes=["Ktb"])
                prog.emit("vector",
                          lambda Eo, h0=h0: Eo.tensor_copy(Sbb[:, :, :], S[:, h0:h0 + 4, :]),
                          reads=[("S", h0 + k2) for k2 in range(4)], writes=["Sbb"])

            def fP(wv_):
                if SLIM < 3:
                    return
                for h2 in range(2):
                    for k in range(4):
                        pbk = 3 + h2
                        mm(PB[pbk][:, k * 128:(k + 1) * 128],
                           [(khb[h2 * 64:(h2 + 1) * 64, k, :],
                             qtb[h2 * 64:(h2 + 1) * 64, k, :])],
                           BK[pbk], ["khb", "qtb"])

            def fTail(wv_):
                if SLIM < 3:
                    return
                h0 = wv_ * 4
                obank = 5 + wv_
                for h2 in range(2 if SLIM >= 32 else 0):
                    pbk = 3 + h2
                    prog.emit("vector",
                              lambda Eo, pbk=pbk, h2=h2: Eo.tensor_mul(
                                  Pmf[:, h2 * 512:(h2 + 1) * 512],
                                  PB[pbk][:, 0:512], tri4[:, :]),
                              reads=[BK[pbk], "tri4"], writes=["Pm"])
                if SLIM < 33:
                    return
                for k in range(4):
                    hp = h0 + k
                    for h2 in range(2):
                        def fn_o(Eo, k=k, hp=hp, h2=h2, tc=tc, obank=obank):
                            sl = slice(h2 * 64, (h2 + 1) * 64)
                            tsl = slice(k * 128, (k + 1) * 128)
                            Eo.matmul(PB[obank][sl, tsl], Sbb[sl, k, :], qtb[sl, k, :],
                                      start=True, stop=False)
                            return Eo.matmul(
                                PB[obank][sl, tsl],
                                vt[:, tc, (2 * hp + h2) * 64:(2 * hp + h2 + 1) * 64],
                                Pm[:, h2 * 4 + k, :], start=False, stop=True)
                        prog.emit("tensor", fn_o,
                                  reads=["Sbb", "qtb", ("vt", tc), "Pm"],
                                  writes=[BK[obank]])
                prog.emit("scalar",
                          lambda Eo, h0=h0, obank=obank: Eo.copy(
                              of[:, h0:h0 + 4, tc * 128:(tc + 1) * 128],
                              PB[obank][:, 0:512].rearrange("p (a b) -> p a b", b=128)),
                          reads=[BK[obank]], writes=[("of", h0 + k2) for k2 in range(4)])
                if SLIM < 5:
                    return
                for k in range(4):
                    hp = h0 + k
                    for h2 in range(2):
                        mm(PB[7][h2 * 64:(h2 + 1) * 64, k * 64:(k + 1) * 64],
                           [(Ktb[:, k, h2 * 64:(h2 + 1) * 64],
                             vt[:, tc, (2 * hp + h2) * 64:(2 * hp + h2 + 1) * 64])],
                           "B7", ["Ktb", ("vt", tc)])
                for k in range(4):
                    hp = h0 + k
                    prog.emit("vector",
                              lambda Eo, k=k, hp=hp: Eo.scalar_tensor_tensor(
                                  S[:, hp, :], S[:, hp, :], lam[:, hp:hp + 1],
                                  PB[7][:, k * 64:(k + 1) * 64], ALU.mult, ALU.add),
                              reads=[("S", hp), "lam", "B7"], writes=[("S", hp)])

            if SLIM >= 2:
                fA(0)
                fES(0)
                fP(0)
                fA(1)
                fTail(0)
                fES(1)
                fP(1)
                fTail(1)
        if ALIM < 5:
            return
        mm(PB[0][0:64, 0:512], [(g1sb[:, c, :], xh[:, c, ts]) for c in range(8)],
           "B0", ["g1sb"] + [("xh", c) for c in range(8)])
        prog.emit("scalar", lambda Eo: Eo.copy(g1a[0:64, :], PB[0][0:64, 0:512]),
                  reads=["B0"], writes=["g1a"])
        for j in range(8):
            pb_, bk_ = PB[j % 4], BK[j % 4]
            scx, kx = [sc0, sc1, sc2, sc3][j % 4], f"sc{j % 4}"
            mm(pb_[:, 0:512], [(g2sb[:, j * 128:(j + 1) * 128], g1a[0:64, :])],
               bk_, ["g2sb", "g1a"])
            prog.emit("scalar", lambda Eo, pb_=pb_, scx=scx: Eo.activation(
                          scx[:, :], pb_[:, 0:512], AF.Sigmoid),
                      reads=[bk_], writes=[kx])
            prog.emit("vector",
                      lambda Eo, j=j, scx=scx: Eo.tensor_mul(og[:, j, :], of[:, j, :],
                                                             scx[:, :]),
                      reads=[("of", j), kx], writes=[("og", j)])
        sqb = kt
        for j in range(8):
            prog.emit("scalar",
                      lambda Eo, j=j: Eo.activation(sqb[:, j // 2, (j % 2) * 512:
                                                        (j % 2) * 512 + 512],
                                                    og[:, j, :], AF.Square),
                      reads=[("og", j)], writes=[("kt", j // 2)])
        def rms_mms(j):
            p = j % 2
            bqa, bqb = 2 + 2 * p, 3 + 2 * p
            sq_ap = sqb[:, j // 2, (j % 2) * 512:(j % 2) * 512 + 512]
            mm(PB[bqa][0:1, 0:512], [(ones_col_b[0:64, :], sq_ap[0:64, :])], BK[bqa],
               [("kt", j // 2), "tri_b"])
            mm(PB[bqb][0:1, 0:512], [(ones_col_b[64:128, :], sq_ap[64:128, :])], BK[bqb],
               [("kt", j // 2), "tri_b"])

        def rms_rows(j):
            p = j % 2
            bqa, bqb = 2 + 2 * p, 3 + 2 * p
            ra_j = rows[0:1, 1024:1536] if p == 0 else rows[0:1, 0:512]
            rb_j = rows[0:1, 1536:2048] if p == 0 else rows[0:1, 512:1024]
            ka = "r_a" if p == 0 else "r_m"
            kb = "r_b" if p == 0 else "r_q"
            prog.emit("scalar", lambda Eo, bqa=bqa, ra_j=ra_j: Eo.activation(
                          ra_j, PB[bqa][0:1, 0:512], AF.Sqrt,
                          bias=cst[0:1, 0:1], scale=1.0 / DH),
                      reads=[BK[bqa], "cst"], writes=[ka])
            prog.emit("scalar", lambda Eo, bqb=bqb, rb_j=rb_j: Eo.activation(
                          rb_j, PB[bqb][0:1, 0:512], AF.Sqrt,
                          bias=cst[0:1, 0:1], scale=1.0 / DH),
                      reads=[BK[bqb], "cst"], writes=[kb])
            prog.emit("vector", lambda Eo, ra_j=ra_j: Eo.reciprocal(ra_j, ra_j),
                      reads=[ka], writes=[ka])
            prog.emit("vector", lambda Eo, rb_j=rb_j: Eo.reciprocal(rb_j, rb_j),
                      reads=[kb], writes=[kb])

        def rms_bc(j):
            p = j % 2
            bbc = 6 + p
            ra_j = rows[0:1, 1024:1536] if p == 0 else rows[0:1, 0:512]
            rb_j = rows[0:1, 1536:2048] if p == 0 else rows[0:1, 512:1024]
            ka = "r_a" if p == 0 else "r_m"
            kb = "r_b" if p == 0 else "r_q"

            def fn_bc(Eo, bbc=bbc, ra_j=ra_j, rb_j=rb_j):
                Eo.matmul(PB[bbc][0:64, 0:512], ones_row_f[0:1, 0:64], ra_j,
                          start=True, stop=True)
                return Eo.matmul(PB[bbc][64:128, 0:512], ones_row_f[0:1, 0:64], rb_j,
                                 start=True, stop=True)
            prog.emit("tensor", fn_bc, reads=[ka, kb, "tri_f"], writes=[BK[bbc]])

        def rms_stt(j):
            p = j % 2
            bbc = 6 + p
            prog.emit("vector",
                      lambda Eo, j=j, bbc=bbc: Eo.scalar_tensor_tensor(
                          on[:, j, :], og[:, j, :], normw_c[:, j:j + 1],
                          PB[bbc][:, 0:512], ALU.mult, ALU.mult),
                      reads=[("og", j), "normw_c", BK[bbc]], writes=[("on", j)])

        for jp in range(4):
            j0, j1 = 2 * jp, 2 * jp + 1
            rms_mms(j0)
            rms_mms(j1)
            rms_rows(j0)
            rms_rows(j1)
            rms_bc(j0)
            rms_bc(j1)
            rms_stt(j0)
            rms_stt(j1)
        for j in range(8):
            s = j % 2
            dma("sync", wat2[:, s, :, :], zo[:, :, j * 128:(j + 1) * 128],
                ["zwo"], [("wat2", s)], f"wat2{s}")
            pb_, bk_ = PB[5 + j % 2], BK[5 + j % 2]
            mm(pb_[:, 0:512], [(wat2[:, s, c, :], on[:, c, :]) for c in range(8)],
               bk_, [("wat2", s)] + [("on", c) for c in range(8)])
            prog.emit("vector",
                      lambda Eo, j=j, ts=ts, pb_=pb_: Eo.tensor_add(xr[:, j, ts],
                                                                    xr[:, j, ts],
                                                                    pb_[:, 0:512]),
                      reads=[bk_, ("xr", j)], writes=[("xr", j)])

    LIMIT = cfg.get("limit", 99)
    for b in range(B):
        prog.emit("vector", lambda Eo: Eo.memset(S[:, :, :], 0.0),
                  reads=[], writes=[("S", hp) for hp in range(8)])
        for blk in range(NBLK):
            bs = slice(blk * TBLK, (blk + 1) * TBLK)
            xv = x_d[b].rearrange("(c p) t -> p c t", p=128)[:, :, bs]
            for c in range(8):
                dma("scalar", xr[:, c, :], xv[:, c, :],
                    [], [("xr", c)], f"xq{c}")
            if LIMIT >= 1:
                emit_ln()
            if LIMIT >= 2:
                emit_ffn_shared()
            if LIMIT >= 3:
                emit_ffn_experts()
            if LIMIT >= 4:
                emit_ln()
            if LIMIT >= 5:
                for sub in range(NSUB):
                    emit_attn_sub(sub)
            if LIMIT >= 6:
                emit_ln()
                emit_ffn_shared()
                emit_ffn_experts()
            yv = y_d[b].rearrange("(c p) t -> p c t", p=128)[:, :, bs]
            for c in range(8):
                dma("scalar", yv[:, c, :], xr[:, c, :],
                    [("xr", c)], [("ystore", b, blk, c)], f"yq{c}")

    for c in range(8):
        nc.scalar.wait_ge(prog.dsem[f"yq{c}"], prog.dcnt[f"yq{c}"])

    ctx.close()
    return nc, prog


def make_inputs(core_x, weights, cfg):
    B, NBLK, TBLK = cfg["B"], cfg["NBLK"], cfg["TBLK"]
    T = NBLK * TBLK
    xt = np.ascontiguousarray(np.transpose(np.asarray(core_x), (0, 2, 1)))
    W = {k: np.asarray(v) for k, v in weights.items()}
    d = {
        "x": xt,
        "w1T": np.ascontiguousarray(np.transpose(W["L0_ffn_w1"], (0, 2, 1))),
        "w3T": np.ascontiguousarray(np.transpose(W["L0_ffn_w3"], (0, 2, 1))),
        "w2T": np.ascontiguousarray(np.transpose(W["L0_ffn_w2"], (0, 2, 1))),
        "sw1T": np.ascontiguousarray(W["L0_ffn_sw1"].T),
        "sw3T": np.ascontiguousarray(W["L0_ffn_sw3"].T),
        "sw2T": np.ascontiguousarray(W["L0_ffn_sw2"].T),
        "gwT": np.ascontiguousarray(W["L0_ffn_gate_w"].T),
        "gbias": np.ascontiguousarray(W["L0_ffn_gate_bias"].reshape(1, NEXP)),
        "wq": np.ascontiguousarray(W["L0_attn_Wq"]),
        "wk": np.ascontiguousarray(W["L0_attn_Wk"]),
        "wv": np.ascontiguousarray(W["L0_attn_Wv"]),
        "wo": np.ascontiguousarray(W["L0_attn_Wo"]),
        "g1": np.ascontiguousarray(W["L0_attn_G1"]),
        "g2": np.ascontiguousarray(W["L0_attn_G2"]),
        "deltar": np.ascontiguousarray(W["L0_attn_delta"].reshape(8, 128)),
        "deltaf": np.ascontiguousarray(W["L0_attn_delta"].reshape(1, E)),
        "normwr": np.ascontiguousarray(W["L0_attn_norm_w"].reshape(8, 128)),
        "ctri": np.triu(np.ones((128, 128), dtype=np.float32)),
        "cidn": np.eye(128, dtype=np.float32),
        "coneh": np.kron(np.eye(NEXP), np.ones((1, 128))).astype(np.float32),
    }
    assert d["x"].shape == (B, E, T), d["x"].shape
    return d


def kernel(**inputs):
    cfg = CFG_FULL
    x_full = np.asarray(inputs["L0_attn_x"])
    n_cores = 8
    nc, _ = build(cfg)
    in_maps = []
    for c in range(n_cores):
        core_x = x_full[c * cfg["B"]:(c + 1) * cfg["B"]]
        in_maps.append(make_inputs(core_x, inputs, cfg))
    res = run_bass_kernel_spmd(nc, in_maps, core_ids=list(range(n_cores)))
    outs = []
    for c in range(n_cores):
        yc = np.asarray(res.results[c]["y"])
        outs.append(np.transpose(yc, (0, 2, 1)))
    return np.ascontiguousarray(np.concatenate(outs, axis=0), dtype=np.float32)
